# version 38
# baseline (speedup 1.0000x reference)
import contextlib
import numpy as np
import ml_dtypes
import concourse.bass as bass
import concourse.mybir as mybir

F32 = mybir.dt.float32
BF16 = mybir.dt.bfloat16
AF = mybir.ActivationFunctionType
ALU = mybir.AluOpType
AX = mybir.AxisListType.X

T, D, H, HD, FF = 2048, 1024, 16, 64, 4096
NT = T // 128
NC = D // 128
NF = FF // 128
GP = 2176

CONST_VALS = [0.0, 1.0, 0.5, -0.5, -1.0, 1e-8, 1e-5]


class Bld:

    ENGS = "TVSGY"

    def __init__(self, nc):
        self.nc = nc
        self.ops = {e: [] for e in self.ENGS}
        self.cnt = {}
        self.wm = {e: {} for e in self.ENGS}
        self.sems = {}
        self.es = contextlib.ExitStack()
        for e in "TVSG":
            self.mksem(e)

    def mksem(self, name):
        if name not in self.sems:
            self.sems[name] = self.es.enter_context(
                self.nc.semaphore("sem_" + name))
            self.cnt[name] = 0
        return name

    def op(self, eng, fn, waits=(), inc=1, kw_sem=None):
        need = {}
        def _add(w):
            if w is None:
                return
            if isinstance(w, (list, tuple)) and w and isinstance(w[0], (list, tuple, type(None))):
                for x in w:
                    _add(x)
                return
            en, val = w
            if val > need.get(en, 0):
                need[en] = val
        for w in waits:
            _add(w)
        final = []
        for en, val in sorted(need.items()):
            if val > self.wm[eng].get(en, 0):
                final.append((en, val))
                self.wm[eng][en] = val
        sem = kw_sem if kw_sem is not None else eng
        self.ops[eng].append((final, fn, inc, sem))
        if inc:
            self.cnt[sem] += inc
            return (sem, self.cnt[sem])
        return None

    def dma(self, eng, sem, out_ap, in_ap, waits=()):
        assert eng in ("Y", "G", "S")
        self.mksem(sem)
        return self.op(eng, lambda h, o=out_ap, i=in_ap: h.dma_start(out=o, in_=i),
                       waits=waits, inc=16, kw_sem=sem)

    def flush(self, block):
        self.replay(block)

    def replay(self, block):
        decos = {"T": block.tensor, "V": block.vector, "S": block.scalar,
                 "G": block.gpsimd, "Y": block.sync}
        for e in self.ENGS:
            ops = self.ops[e]
            if not ops:
                continue
            def section(h, ops=ops, e=e):
                for waits, fn, inc, sem in ops:
                    for en, val in waits:
                        h.wait_ge(self.sems[en], val)
                    ins = fn(h)
                    if inc and ins is not None:
                        ins.then_inc(self.sems[sem], inc)
            decos[e](section)
        self.ops = {e: [] for e in self.ENGS}


def build_core(stages=4, dbg=()):
    nc = bass.Bass("TRN2", target_bir_lowering=False)

    xin = nc.dram_tensor("xin", [T, D], F32, kind="ExternalInput")
    wgT = nc.dram_tensor("wgT", [D, D], F32, kind="ExternalInput")
    wvT = nc.dram_tensor("wvT", [D, D], F32, kind="ExternalInput")
    woT = nc.dram_tensor("woT", [D, D], F32, kind="ExternalInput")
    wfcT = nc.dram_tensor("wfcT", [D, FF], F32, kind="ExternalInput")
    wprojT = nc.dram_tensor("wprojT", [FF, D], F32, kind="ExternalInput")
    bgp = nc.dram_tensor("bgp", [128, NC], F32, kind="ExternalInput")
    bvp = nc.dram_tensor("bvp", [128, NC], F32, kind="ExternalInput")
    bgo = nc.dram_tensor("bgo", [2, D], F32, kind="ExternalInput")
    ones16 = nc.dram_tensor("ones16", [1, 128], BF16, kind="ExternalInput")
    onesf = nc.dram_tensor("onesf", [128, 1], F32, kind="ExternalInput")
    onesp0 = nc.dram_tensor("onesp0", [1, 128], F32, kind="ExternalInput")
    identb = nc.dram_tensor("identb", [128, 128], BF16, kind="ExternalInput")
    onesblk = nc.dram_tensor("onesblk", [128, 128], BF16, kind="ExternalInput")
    maskA = nc.dram_tensor("maskA", [128, 256], F32, kind="ExternalInput")
    mask0 = nc.dram_tensor("mask0", [128, 256], F32, kind="ExternalInput")
    nmask = nc.dram_tensor("nmask", [128, 2], F32, kind="ExternalInput")
    constsI = nc.dram_tensor("constsI", [128, 8], F32, kind="ExternalInput")
    out = nc.dram_tensor("out", [T, D], F32, kind="ExternalOutput")
    x2d = nc.dram_tensor("x2d", [T, D], F32)
    wfcC = nc.dram_tensor("wfcC", [128, NF * 1024], BF16)
    wprojC = nc.dram_tensor("wprojC", [128, NF * 1024], BF16)
    dbg_t = {}
    for name, shp in [("dbg_g", [128, NC * GP]), ("dbg_v", [128, (NT + 1) * 1024]),
                       ("dbg_pv", [128, NC * T]),
                      ("dbg_C", [128, 16 * 256]), ("dbg_h", [128, NF * 512])]:
        if name in dbg:
            dbg_t[name] = nc.dram_tensor(name, shp, F32, kind="ExternalOutput")

    es0 = contextlib.ExitStack()
    sb0 = lambda name, shp, dt: es0.enter_context(nc.sbuf_tensor(name, shp, dt))
    con_sb = sb0("con_sb", [128, 8], F32)
    for i, v in enumerate(CONST_VALS):
        nc.const_aps.register(F32, v, con_sb[:, i:i + 1])
    ident_sb = sb0("ident_sb", [128, 128], BF16)
    oblk_sb = sb0("oblk_sb", [128, 128], BF16)
    maskA_sb = sb0("maskA_sb", [128, 256], F32)
    mask0_sb = sb0("mask0_sb", [128, 256], F32)
    bgp_sb = sb0("bgp_sb", [128, NC], F32)
    bvp_sb = sb0("bvp_sb", [128, NC], F32)
    nmask_sb = sb0("nmask_sb", [128, 2], F32)
    bia_b = sb0("bia_b", [1, 2 * D], BF16)
    o16_sb = sb0("o16_sb", [1, 128], BF16)
    onesf_sb = sb0("onesf_sb", [128, 1], F32)
    op0_sb = sb0("op0_sb", [1, 128], F32)

    b = Bld(nc)
    for dst, src in [(con_sb, constsI), (ident_sb, identb), (oblk_sb, onesblk),
                     (maskA_sb, maskA), (mask0_sb, mask0),
                     (nmask_sb, nmask), (bgp_sb, bgp),
                     (bvp_sb, bvp), (o16_sb, ones16),
                     (onesf_sb, onesf), (op0_sb, onesp0)]:
        tok_pre = b.dma("Y", "pre", dst[:], src[:])
    tok_pre_g = b.dma("G", "preg", bia_b[:], bgo[:])

    block = es0.enter_context(nc.Block())
    es_b2 = contextlib.ExitStack()
    sb2 = lambda name, shp, dt: es_b2.enter_context(nc.sbuf_tensor(name, shp, dt))
    xa = [sb2(f"xa{i}", [128, D], F32) for i in range(2)]
    xsqd = sb2("xsqd", [128, D], BF16)
    x1n = [sb2(f"x1n{i}", [128, D], BF16) for i in range(2)]
    x1T = sb2("x1T", [128, NC * T], BF16)
    woT_sb = sb2("woT_sb", [128, NC * D], BF16)
    ln_st = [{k: sb2(k + str(j), [128, 1], F32) for k in
              ("meanb", "ssqn", "m2b", "varb", "lnvb", "rstdb", "nbb")}
             for j in range(2)]
    ssum = sb2("ssum", [128, NT], F32)
    ssq = sb2("ssq", [128, NT], F32)
    meanv = sb2("meanv", [128, NT], F32)
    rstdv = sb2("rstdv", [128, NT], F32)
    nbv = sb2("nbv", [128, NT], F32)
    varv = sb2("varv", [128, NT], F32)
    es_bb = contextlib.ExitStack()
    sbb = lambda name, shp, dt: es_bb.enter_context(nc.sbuf_tensor(name, shp, dt))
    g_sb = sbb("g_sb", [128, NC * GP], BF16)
    v_sb = sbb("v_sb", [128, (NT + 1) * 1024], BF16)
    rqv_sb = sbb("rqv_sb", [128, 2 * NT], F32)
    es_a2 = contextlib.ExitStack()
    sba2 = lambda name, shp, dt: es_a2.enter_context(nc.sbuf_tensor(name, shp, dt))
    xpair = [sba2(f"xpair{i}", [128, 2 * D], F32) for i in range(2)]
    wstg = sba2("wstg", [128, 2048], BF16)
    wgT_sb = sba2("wgT_sb", [128, NC * D], BF16)
    wvT_sb = sba2("wvT_sb", [128, NC * D], BF16)
    gtk = [sba2(f"gtk{i}", [128, D], BF16) for i in range(2)]
    gsq = [sba2(f"gsq{i}", [128, D], BF16) for i in range(2)]
    rqx = [sba2(f"rqx{i}", [128, D], BF16) for i in range(2)]
    ghat = [sba2(f"ghat{i}", [128, D], BF16) for i in range(2)]
    sba = sba2

    es_psA = contextlib.ExitStack()
    ps = lambda name, shp=(128, 512): es_psA.enter_context(
        nc.psum_tensor(name, list(shp), F32))
    psT = [ps("psT0"), ps("psT1")]
    psG = [ps("psG0"), ps("psG1")]
    psV = [ps("psV0"), ps("psV1")]

    def wload(dst, src, nchunk, sem):
        src3 = src[:].rearrange("(c p) d -> p c d", p=128)
        dst3 = dst[:].rearrange("p (c d) -> p c d", c=nchunk)
        return b.dma("G", sem, dst3, src3)
    tok_wg = wload(wgT_sb, wgT, NC, "wg")
    tok_wv = wload(wvT_sb, wvT, NC, "wv")
    tok_wo = wload(woT_sb, woT, NC, "wo")

    stg_free = None
    for j in range(16):
        lds = []
        for s in range(2):
            f = 2 * j + s
            lds.append(b.dma("G", f"wstg{s}",
                             wstg[:, 1024 * s:1024 * s + 1024].rearrange(
                                 "p (k f) -> p k f", k=8),
                             wfcT[:, 128 * f:128 * f + 128].rearrange(
                                 "(k p) f -> p k f", p=128),
                             waits=[stg_free]))
        wr = b.dma("G", "wcw", wfcC[:, 2048 * j:2048 * (j + 1)], wstg[:],
                   waits=lds)
        stg_free = wr
    for q in range(16):
        ld = b.dma("G", "wstg", wstg[:].rearrange("p (j d) -> p j d", j=2),
                   wprojT[256 * q:256 * (q + 1), :].rearrange(
                       "(j p) d -> p j d", p=128),
                   waits=[stg_free])
        wr = b.dma("G", "wcw", wprojC[:, 2048 * q:2048 * (q + 1)], wstg[:],
                   waits=[ld])
        stg_free = wr
    tok_wcache = stg_free

    g3 = g_sb[:].rearrange("p (c t) -> p c t", c=NC)
    tok_gpad = b.op("V", lambda h, g3=g3: h.memset(g3[:, :, 0:128], 0.0))
    tok_vpad = b.op("V", lambda h: h.memset(v_sb[:, 0:1024], 0.0))

    xa_free = [None, None]
    x1n_free = [None, None]
    psT_free = [None, None]
    x1T_toks = []
    va_toks = []
    xp_free = [None] * 2
    for tp_ in range(NT // 2):
        j = tp_ % 2
        qeng = "Y"
        xsrc = xin[256 * tp_:256 * (tp_ + 1), :].rearrange(
            "(a p) d -> p a d", p=128)
        xdst = xpair[j][:].rearrange("p (a d) -> p a d", a=2)
        tok_x = b.dma(qeng, f"xp{j}", xdst, xsrc, waits=[xp_free[j]])
        for half in range(2):
            t = 2 * tp_ + half
            tok_v1 = b.op("V", lambda h, j=j, half=half, t=t: h.scalar_tensor_tensor(
                xsqd[:], xpair[j][:, 1024 * half:1024 * half + 1024], 1.0,
                xpair[j][:, 1024 * half:1024 * half + 1024], ALU.mult, ALU.mult,
                accum_out=ssq[:, t:t + 1]),
                waits=[tok_x, va_toks[-1][0] if va_toks else None])
            tok_s1 = b.op("S", lambda h, j=j, half=half, t=t: h.activation(
                x1n[half][:], xpair[j][:, 1024 * half:1024 * half + 1024],
                AF.Identity, accum_out=ssum[:, t:t + 1]),
                waits=[tok_x, tok_pre, x1n_free[half],
                       va_toks[-1][1] if va_toks else None])
            x1n_free[half] = tok_s1
            va_toks.append((tok_v1, tok_s1))
        xp_free[j] = va_toks[-1]
        xp_free[j] = (va_toks[-2][0], va_toks[-2][1], va_toks[-1][0],
                      va_toks[-1][1])
    tk = b.op("V", lambda h: h.tensor_scalar_mul(meanv[:], ssum[:], 1.0 / D),
              waits=[va_toks[-1]])
    tk2 = b.op("V", lambda h: h.tensor_scalar_mul(varv[:], ssq[:], 1.0 / D),
               waits=[va_toks[-1], tk])
    tk3 = b.op("V", lambda h: h.tensor_mul(nbv[:], meanv[:], meanv[:]),
               waits=[tk, tk2])
    tk4 = b.op("V", lambda h: h.tensor_sub(varv[:], varv[:], nbv[:]),
               waits=[tk3])
    tk5 = b.op("S", lambda h: h.activation(rstdv[:], varv[:], AF.Ln, bias=1e-5),
               waits=[tk4])
    tk6 = b.op("S", lambda h: h.activation(rstdv[:], rstdv[:], AF.Exp,
                                           scale=-0.5), waits=[tk5])
    tok_stats = b.op("V", lambda h: h.scalar_tensor_tensor(
        nbv[:], meanv[:], -1.0, rstdv[:], ALU.mult, ALU.mult),
        waits=[tk6, tk4])
    for t in range(NT):
        i = t % 2
        tok_x = b.dma("Y", f"xb{i}", xa[i][:], xin[128 * t:128 * (t + 1), :],
                      waits=[xa_free[i]])
        tok_x1n = b.op("S", lambda h, i=i, t=t: h.activation(
            x1n[i][:], xa[i][:], AF.Identity, bias=nbv[:, t:t + 1],
            scale=rstdv[:, t:t + 1]), waits=[tok_x, tok_stats, x1n_free[i]])
        xa_free[i] = tok_x1n
        evs = []
        for j in range(2):
            mmt = None
            for c in range(4 * j, 4 * j + 4):
                mmt = b.op("T", lambda h, i=i, j=j, c=c: h.matmul(
                    psT[j][:, 128 * (c % 4):128 * (c % 4) + 128],
                    x1n[i][:, 128 * c:128 * c + 128], ident_sb[:],
                    start=True, stop=True),
                    waits=[tok_x1n, psT_free[j], tok_pre],
                    inc=1 if c % 4 == 3 else 0)
            pso3 = psT[j][:].rearrange("p (c q) -> p c q", c=4)
            x1T3 = x1T[:].rearrange("p (c t) -> p c t", c=NC)
            ev = b.op("S", lambda h, j=j, t=t, pso3=pso3, x1T3=x1T3: h.activation(
                x1T3[:, 4 * j:4 * j + 4, 128 * t:128 * t + 128], pso3[:],
                AF.Identity), waits=[mmt])
            psT_free[j] = ev
            evs.append(ev)
        x1n_free[i] = evs[-1]
        x1T_toks.append(evs[-1])
        tok_x1T = evs[-1]

    psG_free = [None, None]
    gtk_free = [None, None]
    gsq_free = [None, None]
    rqx_free = [None, None]
    ghat_free = [None, None]
    g_toks = []
    tok_g = None
    for tt in range(NT):
        ev2 = []
        for nh in range(2):
            gi = (tt * 2 + nh) % 2
            mmt = None
            for k in range(NC):
                mmt = b.op("T", lambda h, gi=gi, tt=tt, nh=nh, k=k: h.matmul(
                    psG[gi][:], x1T[:, 2048 * k + 128 * tt:2048 * k + 128 * tt + 128],
                    wgT_sb[:, 1024 * k + 512 * nh:1024 * k + 512 * nh + 512],
                    start=(k == 0), stop=False),
                    waits=[tok_wg, x1T_toks[tt], psG_free[gi]],
                    inc=1 if k == NC - 1 else 0)
            mmt = b.op("T", lambda h, gi=gi, nh=nh: h.matmul(
                psG[gi][:], o16_sb[:], bia_b[0:1, 512 * nh:512 * nh + 512],
                start=False, stop=True), waits=[mmt, tok_pre, tok_pre_g])
            i2 = tt % 2
            e1 = b.op("S", lambda h, gi=gi, i2=i2, nh=nh: h.activation(
                gtk[i2][:, 512 * nh:512 * nh + 512], psG[gi][:], AF.Identity),
                waits=[mmt, gtk_free[i2] if nh == 0 else None])
            e2 = b.op("S", lambda h, gi=gi, i2=i2, nh=nh: h.activation(
                gsq[i2][:, 512 * nh:512 * nh + 512], psG[gi][:], AF.Square),
                waits=[mmt, gsq_free[i2] if nh == 0 else None])
            psG_free[gi] = e2
            ev2.append((e1, e2))
        i2 = tt % 2
        gq3 = gsq[i2][:].rearrange("p (hh d) -> p hh d", hh=16)
        tn = b.op("V", lambda h, tt=tt, gq3=gq3: h.tensor_reduce(
            rqv_sb[:, 16 * (tt % 2):16 * (tt % 2) + 16],
            gq3, AX, ALU.add), waits=[ev2[0][1], ev2[1][1], rqx_free[tt % 2]])
        tn = b.op("S", lambda h, tt=tt: h.activation(
            rqv_sb[:, 16 * (tt % 2):16 * (tt % 2) + 16],
            rqv_sb[:, 16 * (tt % 2):16 * (tt % 2) + 16], AF.Ln), waits=[tn])
        tn = b.op("S", lambda h, tt=tt: h.activation(
            rqv_sb[:, 16 * (tt % 2):16 * (tt % 2) + 16],
            rqv_sb[:, 16 * (tt % 2):16 * (tt % 2) + 16], AF.Exp, scale=-0.5),
            waits=[tn])
        r3o = rqx[i2][:].rearrange("p (hh d) -> p hh d", hh=16)
        r3i = bass.AP(rqv_sb, 16 * (tt % 2), [[2 * NT, 128], [1, 16], [0, 64]])
        tn = b.op("V", lambda h, r3o=r3o, r3i=r3i: h.tensor_copy(r3o, r3i),
                  waits=[tn])
        tg = b.op("V", lambda h, i2=i2: h.tensor_mul(
            ghat[i2][:], gtk[i2][:], rqx[i2][:]),
            waits=[tn, ev2[0][0], ev2[1][0], ghat_free[i2]])
        gtk_free[i2] = tg
        rqx_free[i2] = tg
        gsq_free[i2] = tn
        evs = []
        for j in range(2):
            mmt = None
            for c in range(4 * j, 4 * j + 4):
                mmt = b.op("T", lambda h, i2=i2, j=j, c=c: h.matmul(
                    psT[j][:, 128 * (c % 4):128 * (c % 4) + 128],
                    ghat[i2][:, 128 * c:128 * c + 128], ident_sb[:],
                    start=True, stop=True),
                    waits=[tg, psT_free[j]],
                    inc=1 if c % 4 == 3 else 0)
            pso3 = psT[j][:].rearrange("p (c q) -> p c q", c=4)
            gT3 = g_sb[:].rearrange("p (c t) -> p c t", c=NC)
            ev = b.op("S", lambda h, j=j, tt=tt, pso3=pso3, gT3=gT3: h.activation(
                gT3[:, 4 * j:4 * j + 4, 128 + 128 * tt:128 + 128 * tt + 128],
                pso3[:], AF.Identity), waits=[mmt, tok_gpad if tt == 0 else None])
            psT_free[j] = ev
            evs.append(ev)
        ghat_free[i2] = evs[-1]
        g_toks.append(evs[-1])
        tok_g = evs[-1]

    psV_free = [None, None]
    tok_v = None
    for tt in range(NT):
        for nh in range(2):
            vi = (tt * 2 + nh) % 2
            mmt = None
            for k in range(NC):
                mmt = b.op("T", lambda h, vi=vi, tt=tt, nh=nh, k=k: h.matmul(
                    psV[vi][:], x1T[:, 2048 * k + 128 * tt:2048 * k + 128 * tt + 128],
                    wvT_sb[:, 1024 * k + 512 * nh:1024 * k + 512 * nh + 512],
                    start=(k == 0), stop=(k == NC - 1)),
                    waits=[tok_wv, x1T_toks[tt], psV_free[vi]],
                    inc=1 if k == NC - 1 else 0)
            ev = b.op("S", lambda h, vi=vi, tt=tt, nh=nh: h.activation(
                v_sb[:, 1024 * (tt + 1) + 512 * nh:1024 * (tt + 1) + 512 * nh + 512],
                psV[vi][:], AF.Identity), waits=[mmt, tok_vpad])
            psV_free[vi] = ev
            tok_v = ev

    def dump(name, src_sb, tok, width):
        if name not in dbg_t:
            return
        buf = sba(name + "_b", [128, 512], F32)
        td = None
        for c in range(width // 512):
            tk = b.op("V", lambda h, c=c: h.tensor_copy(
                buf[:], src_sb[:, 512 * c:512 * c + 512]), waits=[tok, td])
            td = b.dma("Y", "dbg", dbg_t[name][:, 512 * c:512 * c + 512],
                       buf[:], waits=[tk])
    dump("dbg_g", g_sb, tok_g, NC * GP // 512 * 512)
    dump("dbg_v", v_sb, tok_v, (NT + 1) * 1024)


    if stages < 2:
        b.op("Y", lambda h: None,
             waits=[(s, b.cnt[s]) for s in b.cnt if s not in "TVS"], inc=0)
        b.flush(block)
        return nc

    b.flush(block)
    es_a2.close()
    es_attR = contextlib.ExitStack()
    sba = lambda name, shp, dt: es_attR.enter_context(
        nc.sbuf_tensor(name, shp, dt, side="right"))
    pv_sb = x1T
    C_sb = [sba(f"C{i}", [128, 16 * 256], F32) for i in range(2)]
    geneb = [sba(f"gene{i}", [128, NT], F32) for i in range(2)]
    etotb = [sba(f"etot{i}", [128, NT], F32) for i in range(2)]
    efb = [sba(f"ef{i}", [128, NT], F32) for i in range(2)]
    denomb = sba("denomb", [128, NT], F32)
    invb = sba("invb", [128, NT], F32)
    invsb = sba("invsb", [128, 1], F32)
    s1b = sba("s1b", [1, 1], F32)
    sib = sba("sib", [128, 1], F32)
    rsib = sba("rsib", [128, 1], F32)
    fitb = [sba(f"fit{i}", [128, NT], F32) for i in range(2)]
    rqb = [sba(f"rqb{i}", [128, 2], F32) for i in range(2)]
    s_sb = [sba(f"s_sb{i}", [128, 2], F32) for i in range(2)]
    ebf = [sba(f"ebf{i}", [128, 256], F32) for i in range(4)]
    em_sb = [sba(f"em{i}", [128, 256], BF16) for i in range(4)]
    PT_sb = [sba(f"PT{i}", [128, 512], BF16) for i in range(2)]

    es_psA.close()
    es_psB = contextlib.ExitStack()
    psb_ = lambda name, shp=(128, 512): es_psB.enter_context(
        nc.psum_tensor(name, list(shp), F32))
    psD = [psb_("psD0", (128, 256)), psb_("psD1", (128, 256))]
    psPT = [psb_("psPT0", (128, 512)), psb_("psPT1", (128, 512))]
    psPV = [psb_("psPV0", (128, 256)), psb_("psPV1", (128, 256))]
    psS = psb_("psS", (128, 512))

    entryB = tuple((s, b.cnt[s]) for s in b.cnt)

    psD_free = [entryB, entryB]
    psPT_free = [entryB] * 4
    psPV_free = [entryB] * 4
    C_free = [None, None]
    gene_free = [None, None]
    etot_free = [None, None]
    fit_free = [None, None]
    ef_free = [None, None]
    ebf_free = [None] * 4
    em_free = [None] * 4
    rq_tok = [None, None]
    PTs_free = [None] * 4
    sq_free = [None, None]
    stats_ps_free = entryB
    pv_toks = []
    tok_nr = [None, None]
    tok_C_h = [None, None]
    tok_fit_h = [None, None]
    c_last = [None, None]
    stat_sm_free = [None]
    pt_mm = [None] * 4
    pv_mm = [None] * 4
    pt_ev = [None, None]
    e_tok = [None] * 4
    em_tok = [None] * 4

    def emit_pass1(h_i, qt):
        hb = h_i % 2
        hc, hp = h_i // 2, 64 * (h_i % 2)
        qb = qt % 2
        md = b.op("T", lambda h, hc=hc, hp=hp, qt=qt, qb=qb: h.matmul(
            psD[qb][:], g_sb[hp:hp + 64, GP * hc + 128 + 128 * qt:GP * hc + 256 + 128 * qt],
            g_sb[hp:hp + 64, GP * hc + 128 * qt:GP * hc + 128 * qt + 256],
            start=True, stop=True),
            waits=[g_toks[qt], tok_gpad, psD_free[qb]])
        t4 = b.op("V", lambda h, hb=hb, qt=qt, qb=qb,
                  msk=(mask0_sb if qt == 0 else maskA_sb): h.scalar_tensor_tensor(
            C_sb[hb][:, 256 * qt:256 * qt + 256], psD[qb][:], 1.0,
            msk[:], ALU.mult, ALU.mult,
            accum_out=geneb[hb][:, qt:qt + 1]),
            waits=[md, C_free[hb] if qt == 0 else None,
                   gene_free[hb] if qt == 0 else None])
        psD_free[qb] = t4
        c_last[hb] = t4

    def emit_stats(h_i):
        nonlocal stats_ps_free
        hb = h_i % 2
        tk = b.op("V", lambda h, hb=hb: h.tensor_scalar(
            denomb[:], geneb[hb][:], 1.0 / 4096.0, 1.0, ALU.mult, ALU.add),
            waits=[c_last[hb], stat_sm_free[0]])
        gene_free[hb] = tk
        tk = b.op("V", lambda h: h.reciprocal(invb[:], denomb[:]), waits=[tk])
        tok_inv = tk
        tk = b.op("V", lambda h: h.reduce_sum(invsb[:], invb[:], AX), waits=[tk])
        m1 = b.op("T", lambda h: h.matmul(psS[0:1, 0:1], invsb[:],
                                          onesf_sb[:], start=True, stop=True),
                  waits=[tk, stats_ps_free, tok_pre])
        e1 = b.op("S", lambda h: h.activation(s1b[:], psS[0:1, 0:1],
                                              AF.Identity), waits=[m1])
        m2 = b.op("T", lambda h: h.matmul(psS[:, 1:2], op0_sb[:], s1b[:],
                                          start=True, stop=True), waits=[e1])
        e2 = b.op("S", lambda h: h.activation(sib[:], psS[:, 1:2],
                                              AF.Identity), waits=[m2])
        stats_ps_free = e2
        tk = b.op("V", lambda h: h.reciprocal(rsib[:], sib[:]), waits=[e2])
        tok_fit_h[hb] = b.op("V", lambda h, hb=hb: h.tensor_scalar(
            fitb[hb][:], invb[:], rsib[:, 0:1], 0.5, ALU.mult, ALU.mult),
            waits=[tk, tok_inv, fit_free[hb]])
        ef_t = b.op("S", lambda h, hb=hb: h.activation(
            efb[hb][:], fitb[hb][:], AF.Exp), waits=[tok_fit_h[hb], ef_free[hb]])
        tok_fit_h[hb] = (tok_fit_h[hb], ef_t)
        stat_sm_free[0] = tok_fit_h[hb]
        tok_C_h[hb] = c_last[hb]

    def emit_e(h_i, qt):
        hb = h_i % 2
        q4 = qt % 4
        e = b.op("S", lambda h, hb=hb, qt=qt, q4=q4: h.activation(
            ebf[q4][:], C_sb[hb][:, 256 * qt:256 * qt + 256], AF.Exp,
            scale=fitb[hb][:, qt:qt + 1], bias=fitb[hb][:, qt:qt + 1],
            accum_out=etotb[hb][:, qt:qt + 1]),
            waits=[tok_fit_h[hb], tok_C_h[hb], ebf_free[q4],
                   etot_free[hb] if qt == 0 else None])
        e_tok[q4] = e
        if qt == NT - 1:
            C_free[hb] = e
            fit_free[hb] = e

    def emit_srq(h_i, qt):
        hb = h_i % 2
        p2 = (qt // 2) % 2
        if qt == 0:
            t1 = b.op("V", lambda h, hb=hb, p2=p2: h.scalar_tensor_tensor(
                s_sb[p2][:, 0:1], efb[hb][:, 0:1], nmask_sb[:, 1:2],
                etotb[hb][:, 0:1], ALU.mult, ALU.add),
                waits=[e_tok[1], sq_free[p2]])
            ts = b.op("V", lambda h, hb=hb, p2=p2: h.scalar_tensor_tensor(
                s_sb[p2][:, 1:2], efb[hb][:, 1:2], nmask_sb[:, 0:1],
                etotb[hb][:, 1:2], ALU.mult, ALU.add),
                waits=[e_tok[1], t1])
        else:
            ts = b.op("V", lambda h, hb=hb, qt=qt, p2=p2: h.scalar_tensor_tensor(
                s_sb[p2][:], efb[hb][:, qt:qt + 2], nmask_sb[:, 0:1],
                etotb[hb][:, qt:qt + 2], ALU.mult, ALU.add),
                waits=[e_tok[(qt + 1) % 4], sq_free[p2]])
        trq = b.op("V", lambda h, p2=p2: h.reciprocal(
            rqb[p2][:], s_sb[p2][:]), waits=[ts])
        rq_tok[p2] = trq
        if qt == NT - 2:
            ef_free[hb] = ts
            etot_free[hb] = ts

    def emit_em(h_i, qt):
        q4 = qt % 4
        p2 = (qt // 2) % 2
        msk = mask0_sb if qt == 0 else maskA_sb
        tem = b.op("V", lambda h, q4=q4, p2=p2, qt=qt, msk=msk: h.scalar_tensor_tensor(
            em_sb[q4][:], ebf[q4][:], rqb[p2][:, qt % 2:qt % 2 + 1], msk[:],
            ALU.mult, ALU.mult), waits=[rq_tok[p2], e_tok[q4], em_free[q4]])
        ebf_free[q4] = tem
        sq_free[p2] = tem
        em_tok[q4] = tem

    def emit_pt(h_i, qt):
        q4 = qt % 4
        pb = q4 // 2
        mpt = None
        for half in range(2):
            mpt = b.op("T", lambda h, q4=q4, pb=pb, half=half: h.matmul(
                psPT[pb][:, 256 * (q4 % 2) + 128 * half:256 * (q4 % 2) + 128 * half + 128],
                em_sb[q4][:, 128 * half:128 * half + 128], ident_sb[:],
                start=True, stop=True),
                waits=[em_tok[q4], psPT_free[q4]], inc=half)
        em_free[q4] = mpt
        pt_mm[q4] = mpt
        if q4 % 2 == 1:
            ept = b.op("S", lambda h, pb=pb: h.activation(
                PT_sb[pb][:], psPT[pb][:], AF.Identity),
                waits=[mpt, pt_mm[q4 - 1], PTs_free[2 * pb], PTs_free[2 * pb + 1]])
            psPT_free[q4] = ept
            psPT_free[q4 - 1] = ept
            pt_ev[pb] = ept

    def emit_pv(h_i, qt):
        hb = h_i % 2
        hc, hp = h_i // 2, 64 * (h_i % 2)
        q4 = qt % 4
        pb = q4 // 2
        mpv = None
        for half in range(2):
            mpv = b.op("T", lambda h, h_i=h_i, hp=hp, qt=qt, q4=q4, pb=pb, half=half: h.matmul(
                psPV[pb][hp:hp + 64, 128 * (q4 % 2):128 * (q4 % 2) + 128],
                v_sb[:, 1024 * (qt + half) + 64 * h_i:1024 * (qt + half) + 64 * h_i + 64],
                PT_sb[pb][:, 256 * (q4 % 2) + 128 * half:256 * (q4 % 2) + 128 * half + 128],
                start=(half == 0), stop=(half == 1),
                tile_position=(0, hp)),
                waits=[pt_ev[pb], tok_v, psPV_free[q4]], inc=half)
        PTs_free[q4] = mpv
        if q4 % 2 == 1:
            epv = b.op("S", lambda h, hb=hb, hc=hc, hp=hp, qt=qt, pb=pb: h.activation(
                pv_sb[hp:hp + 64, 2048 * hc + 128 * (qt - 1):2048 * hc + 128 * (qt + 1)],
                psPV[pb][hp:hp + 64, :], AF.Identity,
                bias=bvp_sb[hp:hp + 64, hc:hc + 1]),
                waits=[mpv, pv_mm[q4 - 1]])
            psPV_free[q4 - 1] = epv
            psPV_free[q4] = epv
            pv_toks.append(epv)
        pv_mm[q4] = mpv

    NSTEP = H * NT
    def hq(j):
        return j // NT, j % NT
    for i in range(NSTEP + 23):
        if i < NSTEP:
            emit_pass1(*hq(i))
        if i >= 22 and i - 22 < NSTEP:
            emit_pv(*hq(i - 22))
        if i >= 17 and i - 17 < NSTEP:
            emit_e(*hq(i - 17))
            if (i - 17) % NT % 2 == 1:
                emit_srq(hq(i - 17)[0], hq(i - 17)[1] - 1)
        if i >= 19 and i - 19 < NSTEP:
            emit_em(*hq(i - 19))
        if i >= 20 and i - 20 < NSTEP:
            emit_pt(*hq(i - 20))
        if i < NSTEP and hq(i)[1] == NT - 1:
            emit_stats(hq(i)[0])
    tok_pv = pv_toks[-1]


    if "dbg_pv" in dbg_t:
        buf = sba("dbg_pv_b", [128, 512], F32)
        td = None
        for c in range(NC * T // 512):
            tk = b.op("V", lambda h, c=c: h.tensor_copy(
                buf[:], pv_sb[:, 512 * c:512 * c + 512]), waits=[tok_pv, td])
            td = b.dma("Y", "dbg", dbg_t["dbg_pv"][:, 512 * c:512 * c + 512],
                       buf[:], waits=[tk])
    if "dbg_C" in dbg_t:
        b.dma("Y", "dbg", dbg_t["dbg_C"][:], C_sb[1][:], waits=[tok_pv])

    if stages < 3:
        b.op("Y", lambda h: None, waits=[(s, b.cnt[s]) for s in b.cnt
                                          if s not in "TVS"] + [tok_pv], inc=0)
        b.flush(block)
        return nc

    b.flush(block)
    es_attR.close()
    es_bb.close()
    es_mR = contextlib.ExitStack()
    sba = lambda name, shp, dt: es_mR.enter_context(
        nc.sbuf_tensor(name, shp, dt, side="right"))
    x2nT = sba("x2nT", [128, NC * T], BF16)
    x2t = [sba(f"x2t{i}", [128, D], F32) for i in range(2)]
    wfcb = [sba(f"wfcb{i}", [128, 2048], BF16) for i in range(2)]
    wpb = [sba(f"wpb{i}", [128, 2048], BF16) for i in range(2)]
    h_sb = sba("h_sb", [128, NF * 512], BF16)
    frb = [sba(f"frb{i}", [128, 512], F32) for i in range(2)]
    y_b = [sba(f"y{i}", [128, D], F32) for i in range(4)]
    t1b = sba("t1b", [128, D], F32)
    memb = sba("memb", [128, D], F32)
    spkA = sba("spkA", [128, D], F32)
    spkB = sba("spkB", [128, D], F32)
    accb = sba("accb", [128, D], F32)
    x2r = [sba(f"x2r{i}", [128, D], F32) for i in range(2)]

    xr = xa

    es_psB.close()
    es_psM = contextlib.ExitStack()
    psY = [es_psM.enter_context(nc.psum_tensor(f"psY{i}", [128, 512], F32))
           for i in range(8)]
    entryM = tuple((s, b.cnt[s]) for s in b.cnt)
    x2n = x1n

    xr_free = [entryM, entryM]
    x2t_free = [entryM, entryM]
    x2n_free = [entryM, entryM]
    psY_free = [entryM] * 8
    st2_free = [entryM, entryM]
    x2n_tok = [None, None]
    wfcb_free = [entryM, entryM]
    wpb_free = [entryM, entryM]
    frb_free = [entryM, entryM]
    h_free = entryM
    y_free = [entryM] * 4
    x2r_free = [entryM, entryM]
    scan_free = entryM
    x2nT_toks = []
    x2d_toks = []
    h_toks = [None] * NF
    pend_outs = []

    def emit_b2_tile(tt):
        i = tt % 2
        M = ln_st[i]
        tok_xr = b.dma("Y", f"xr{i}", xr[i][:], xin[128 * tt:128 * (tt + 1), :],
                       waits=[xr_free[i]])
        tk_nh = []
        for nh in range(2):
            pw = 2 + 2 * i + nh
            mmt = None
            for c in range(NC):
                mmt = b.op("T", lambda h, pw=pw, c=c, tt=tt, nh=nh: h.matmul(
                    psY[pw][:], pv_sb[:, 2048 * c + 128 * tt:2048 * c + 128 * tt + 128],
                    woT_sb[:, 1024 * c + 512 * nh:1024 * c + 512 * nh + 512],
                    start=(c == 0), stop=False),
                    waits=[tok_pv, tok_wo, psY_free[pw]],
                    inc=1 if c == NC - 1 else 0)
            mmt = b.op("T", lambda h, pw=pw, nh=nh: h.matmul(
                psY[pw][:], o16_sb[:], bia_b[0:1, D + 512 * nh:D + 512 * nh + 512],
                start=False, stop=True), waits=[mmt, tok_pre_g])
            tv = b.op("V", lambda h, pw=pw, i=i, nh=nh: h.tensor_add(
                x2t[i][:, 512 * nh:512 * nh + 512], psY[pw][:],
                xr[i][:, 512 * nh:512 * nh + 512]),
                waits=[mmt, tok_xr, x2t_free[i] if nh == 0 else None])
            psY_free[pw] = tv
            tk_nh.append(tv)
        xr_free[i] = tk_nh[-1]
        tok_x2 = tk_nh[-1]
        tok_x2w = b.dma("Y", f"x2w{i}", x2d[128 * tt:128 * (tt + 1), :],
                        x2t[i][:], waits=[tok_x2])
        x2d_toks.append(tok_x2w)
        tok_v1 = b.op("V", lambda h, i=i, tt=tt: h.scalar_tensor_tensor(
            xsqd[:], x2t[i][:], 1.0, x2t[i][:], ALU.mult, ALU.mult,
            accum_out=ssq[:, tt:tt + 1]), waits=[tok_x2, entryM])
        tok_s1 = b.op("S", lambda h, i=i, tt=tt: h.activation(
            x2n[i][:], x2t[i][:], AF.Identity, accum_out=ssum[:, tt:tt + 1]),
            waits=[tok_x2, x2n_free[i]])
        tk = b.op("V", lambda h, tt=tt, M=M: h.tensor_scalar_mul(
            M["meanb"][:], ssum[:, tt:tt + 1], 1.0 / D),
            waits=[tok_s1, st2_free[i]])
        tk = b.op("V", lambda h, tt=tt, M=M: h.tensor_scalar_mul(
            M["ssqn"][:], ssq[:, tt:tt + 1], 1.0 / D), waits=[tok_v1, tk])
        tk = b.op("V", lambda h, M=M: h.tensor_mul(M["m2b"][:], M["meanb"][:],
                                                   M["meanb"][:]), waits=[tk])
        tok_var = b.op("V", lambda h, M=M: h.tensor_sub(M["varb"][:], M["ssqn"][:],
                                                        M["m2b"][:]), waits=[tk])
        tk = b.op("S", lambda h, M=M: h.activation(M["lnvb"][:], M["varb"][:], AF.Ln,
                                                   bias=1e-5), waits=[tok_var])
        tok_rstd = b.op("S", lambda h, M=M: h.activation(M["rstdb"][:], M["lnvb"][:],
                                                         AF.Exp, scale=-0.5), waits=[tk])
        tok_nb = b.op("V", lambda h, M=M: h.scalar_tensor_tensor(
            M["nbb"][:], M["meanb"][:], -1.0, M["rstdb"][:], ALU.mult, ALU.mult),
            waits=[tok_rstd])
        tok_x2n = b.op("S", lambda h, i=i, M=M: h.activation(
            x2n[i][:], x2t[i][:], AF.Identity, bias=M["nbb"][:, 0:1],
            scale=M["rstdb"][:, 0:1]), waits=[tok_nb, x2n_free[i], tok_s1])
        st2_free[i] = tok_x2n
        x2t_free[i] = (tok_x2n, tok_x2w)
        x2n_tok[i] = tok_x2n
        last = [tt - 1] if tt > 0 else []
        if tt % 4 == 3:
            last = last + [tt]
        for lt in last:
            li = lt % 2
            evs = []
            for j in range(2):
                pj = 6 + j
                mmt = None
                for c in range(4 * j, 4 * j + 4):
                    mmt = b.op("T", lambda h, li=li, pj=pj, c=c: h.matmul(
                        psY[pj][:, 128 * (c % 4):128 * (c % 4) + 128],
                        x2n[li][:, 128 * c:128 * c + 128], ident_sb[:],
                        start=True, stop=True),
                        waits=[x2n_tok[li], psY_free[pj]],
                        inc=1 if c % 4 == 3 else 0)
                pso3 = psY[pj][:].rearrange("p (c q) -> p c q", c=4)
                x2T3 = x2nT[:].rearrange("p (c t) -> p c t", c=NC)
                ev = b.op("S", lambda h, j=j, lt=lt, pso3=pso3, x2T3=x2T3: h.activation(
                    x2T3[:, 4 * j:4 * j + 4, 128 * lt:128 * lt + 128], pso3[:],
                    AF.Identity), waits=[mmt])
                psY_free[6 + j] = ev
                evs.append(ev)
            x2n_free[li] = evs[-1]
            x2nT_toks.append(evs[-1])

    def emit_fc(gi):
        tok_wf = [None, None]
        for f in range(NF):
            pf = f % 2
            fp = f // 2
            if f % 2 == 0:
                tok_wf[fp % 2] = b.dma("G", f"wfs{fp % 2}", wfcb[fp % 2][:],
                                       wfcC[:, 2048 * fp:2048 * (fp + 1)],
                                       waits=[wfcb_free[fp % 2], tok_wcache])
            mmt = None
            for k in range(NC):
                mmt = b.op("T", lambda h, f=f, fp=fp, k=k, gi=gi: h.matmul(
                    psY[f % 2][:],
                    wfcb[fp % 2][:, 1024 * (f % 2) + 128 * k:1024 * (f % 2) + 128 * k + 128],
                    x2nT[:, 2048 * k + 512 * gi:2048 * k + 512 * gi + 512],
                    start=(k == 0), stop=(k == NC - 1)),
                    waits=[tok_wf[fp % 2], x2nT_toks[-1], psY_free[f % 2]],
                    inc=1 if k == NC - 1 else 0)
            if f % 2 == 1:
                wfcb_free[fp % 2] = mmt
            pf = f % 2
            tr = b.op("S", lambda h, pf=pf: h.activation(
                frb[pf][:], psY[pf][:], AF.Relu), waits=[mmt, frb_free[pf]])
            psY_free[pf] = tr
            th = b.op("S", lambda h, pf=pf, f=f: h.activation(
                h_sb[:, 512 * f:512 * f + 512], frb[pf][:], AF.Square),
                waits=[tr, h_free])
            frb_free[pf] = th
            h_toks[f] = th

    def emit_proj_scan(gi):
        nonlocal scan_free, h_free
        while pend_outs:
            tt0, ob0, tout0 = pend_outs.pop(0)
            tst = b.dma("Y", f"out{tt0}", out[128 * tt0:128 * (tt0 + 1), :],
                        ob0[:], waits=[tout0])
            y_free[tt0 % 4] = tst
        pj_done = []
        tok_wp = [None, None]
        for f in range(NF):
            f2 = f // 2
            if f % 2 == 0:
                tok_wp[f2 % 2] = b.dma(
                    "G", f"wps{f2 % 2}", wpb[f2 % 2][:],
                    wprojC[:, 2048 * f2:2048 * (f2 + 1)],
                    waits=[wpb_free[f2 % 2], tok_wcache])
            last = None
            for tti in range(4):
                for nh in range(2):
                    pj = 2 * tti + nh
                    last = b.op("T", lambda h, f=f, f2=f2, tti=tti, nh=nh, pj=pj: h.matmul(
                        psY[pj][:], h_sb[:, 512 * f + 128 * tti:512 * f + 128 * tti + 128],
                        wpb[f2 % 2][:, 1024 * (f % 2) + 512 * nh:1024 * (f % 2) + 512 * nh + 512],
                        start=(f == 0), stop=(f == NF - 1)),
                        waits=[tok_wp[f2 % 2], h_toks[f],
                               psY_free[pj] if f == 0 else None],
                        inc=1 if (f == NF - 1 or (tti == 3 and nh == 1)) else 0)
                    if f == NF - 1:
                        pj_done.append(last)
            if f % 2 == 1:
                wpb_free[f2 % 2] = last
        h_free = ("T", b.cnt["T"])
        for tti in range(4):
            tt = 4 * gi + tti
            i = tt % 2
            i4 = tt % 4
            ev = []
            for nh in range(2):
                pj = 2 * tti + nh
                e = b.op("S", lambda h, pj=pj, i4=i4, nh=nh: h.activation(
                    y_b[i4][:, 512 * nh:512 * nh + 512], psY[pj][:],
                    AF.Identity), waits=[pj_done[pj], y_free[i4]])
                psY_free[pj] = e
                ev.append(e)
            tok_y = ev[-1]
            tok_x2r = b.dma("Y", f"x2r{i}", x2r[i][:],
                            x2d[128 * tt:128 * (tt + 1), :],
                            waits=[x2d_toks[tt], x2r_free[i]])
            y = y_b[i4]
            tk = b.op("V", lambda h, y=y: h.tensor_single_scalar(
                spkA[:], y[:], 1.0, ALU.is_gt), waits=[tok_y, scan_free])
            tmem = None
            acc_t = None
            for step in range(2, 6):
                s_cur = spkB if step % 2 == 0 else spkA
                s_prv = spkA if step % 2 == 0 else spkB
                src_ = y if step == 2 else memb
                tk = b.op("V", lambda h, src_=src_, y=y: h.scalar_tensor_tensor(
                    t1b[:], src_[:], 0.9, y[:], ALU.mult, ALU.add),
                    waits=[tk, tmem])
                tmem = b.op("V", lambda h, s_prv=s_prv: h.tensor_sub(
                    memb[:], t1b[:], s_prv[:]), waits=[tk])
                tk = b.op("V", lambda h, s_cur=s_cur: h.tensor_single_scalar(
                    s_cur[:], memb[:], 1.0, ALU.is_gt), waits=[tmem])
                if step == 2:
                    acc_t = b.op("V", lambda h: h.tensor_add(
                        accb[:], spkA[:], spkB[:]), waits=[tk])
                else:
                    acc_t = b.op("V", lambda h, s_cur=s_cur: h.tensor_add(
                        accb[:], accb[:], s_cur[:]), waits=[tk, acc_t])
            ob = y_b[i4]
            tout = b.op("V", lambda h, i=i, ob=ob: h.scalar_tensor_tensor(
                ob[:], accb[:], 0.2, x2r[i][:], ALU.mult, ALU.add),
                waits=[acc_t, tok_x2r])
            scan_free = tout
            x2r_free[i] = tout
            pend_outs.append((tt, ob, tout))

    for tt in range(4):
        emit_b2_tile(tt)
    emit_fc(0)
    for tt in range(4, 8):
        emit_b2_tile(tt)
    emit_proj_scan(0)
    emit_fc(1)
    for tt in range(8, 12):
        emit_b2_tile(tt)
    emit_proj_scan(1)
    emit_fc(2)
    for tt in range(12, 16):
        emit_b2_tile(tt)
    emit_proj_scan(2)
    emit_fc(3)
    emit_proj_scan(3)
    while pend_outs:
        tt0, ob0, tout0 = pend_outs.pop(0)
        tst = b.dma("Y", f"out{tt0}", out[128 * tt0:128 * (tt0 + 1), :],
                    ob0[:], waits=[tout0])
        y_free[tt0 % 4] = tst


    b.op("Y", lambda h: None,
         waits=[(s, b.cnt[s]) for s in b.cnt if s not in "TVS"], inc=0)
    b.flush(block)
    return nc


def prep_inputs(xb, Wg, bg, Wv, bv, Wo, bo, W_fc, W_proj):
    bf = ml_dtypes.bfloat16
    f32 = np.float32
    consts = np.tile(np.array([CONST_VALS + [0.0]], dtype=f32), (128, 1))
    r = np.arange(128)[:, None]
    c = np.arange(256)[None, :]
    mA = ((c >= r) & (c <= r + 128)).astype(f32)
    m0 = mA.copy()
    m0[:, :128] = 0.0
    oblk = np.zeros((128, 128), dtype=np.float32)
    for dc in range(NC):
        for p in range(128):
            oblk[p, 16 * dc + 2 * dc + (1 if p >= 64 else 0)] = 1.0
    return {
        "xin": np.ascontiguousarray(xb, dtype=f32),
        "wgT": np.ascontiguousarray(Wg.T, dtype=f32),
        "wvT": np.ascontiguousarray(Wv.T, dtype=f32),
        "woT": np.ascontiguousarray(Wo.T, dtype=f32),
        "wfcT": np.ascontiguousarray(W_fc.T, dtype=f32),
        "wprojT": np.ascontiguousarray(W_proj.T, dtype=f32),
        "bgp": np.ascontiguousarray(bg.reshape(NC, 128).T, dtype=f32),
        "bvp": np.ascontiguousarray(
            bv.reshape(NC, 2, 64).transpose(1, 2, 0).reshape(128, NC), dtype=f32),
        "bgo": np.ascontiguousarray(np.stack([bg, bo]), dtype=f32),
        "ones16": np.ones((1, 128), dtype=bf),
        "onesf": np.ones((128, 1), dtype=f32),
        "onesp0": np.ones((1, 128), dtype=f32),
        "identb": np.eye(128).astype(bf),
        "onesblk": oblk.astype(bf),
        "maskA": mA, "mask0": m0,
        "nmask": np.stack([np.full(128, -127.0, np.float32),
                           -(255.0 - np.arange(128, dtype=np.float32))],
                          axis=1),
        "constsI": consts,
    }




from concourse.bass_utils import run_bass_kernel_spmd


def kernel(L0_attn_x, L0_attn_Wg, L0_attn_bg, L0_attn_Wv, L0_attn_bv,
           L0_attn_Wo, L0_attn_bo, L0_ffn_W_fc, L0_ffn_W_proj):
    x = np.ascontiguousarray(np.asarray(L0_attn_x), dtype=np.float32)
    base = prep_inputs(x[0], np.asarray(L0_attn_Wg), np.asarray(L0_attn_bg),
                       np.asarray(L0_attn_Wv), np.asarray(L0_attn_bv),
                       np.asarray(L0_attn_Wo), np.asarray(L0_attn_bo),
                       np.asarray(L0_ffn_W_fc), np.asarray(L0_ffn_W_proj))
    in_maps = []
    for c in range(8):
        m = dict(base)
        m["xin"] = x[c]
        in_maps.append(m)
    nc = build_core(stages=4, dbg=())
    res = run_bass_kernel_spmd(nc, in_maps, core_ids=list(range(8)))
    return np.stack([np.asarray(r["out"], dtype=np.float32)
                     for r in res.results], axis=0)


# version 39
# speedup vs baseline: 1.0179x; 1.0179x over previous
import contextlib
import numpy as np
import ml_dtypes
import concourse.bass as bass
import concourse.mybir as mybir

F32 = mybir.dt.float32
BF16 = mybir.dt.bfloat16
AF = mybir.ActivationFunctionType
ALU = mybir.AluOpType
AX = mybir.AxisListType.X

T, D, H, HD, FF = 2048, 1024, 16, 64, 4096
NT = T // 128
NC = D // 128
NF = FF // 128
GP = 2176

CONST_VALS = [0.0, 1.0, 0.5, -0.5, -1.0, 1e-8, 1e-5]


class Bld:

    ENGS = "TVSGY"

    def __init__(self, nc):
        self.nc = nc
        self.ops = {e: [] for e in self.ENGS}
        self.cnt = {}
        self.wm = {e: {} for e in self.ENGS}
        self.sems = {}
        self.es = contextlib.ExitStack()
        for e in "TVSG":
            self.mksem(e)

    def mksem(self, name):
        if name not in self.sems:
            self.sems[name] = self.es.enter_context(
                self.nc.semaphore("sem_" + name))
            self.cnt[name] = 0
        return name

    def op(self, eng, fn, waits=(), inc=1, kw_sem=None):
        need = {}
        def _add(w):
            if w is None:
                return
            if isinstance(w, (list, tuple)) and w and isinstance(w[0], (list, tuple, type(None))):
                for x in w:
                    _add(x)
                return
            en, val = w
            if val > need.get(en, 0):
                need[en] = val
        for w in waits:
            _add(w)
        final = []
        for en, val in sorted(need.items()):
            if val > self.wm[eng].get(en, 0):
                final.append((en, val))
                self.wm[eng][en] = val
        sem = kw_sem if kw_sem is not None else eng
        self.ops[eng].append((final, fn, inc, sem))
        if inc:
            self.cnt[sem] += inc
            return (sem, self.cnt[sem])
        return None

    def dma(self, eng, sem, out_ap, in_ap, waits=()):
        assert eng in ("Y", "G", "S")
        self.mksem(sem)
        return self.op(eng, lambda h, o=out_ap, i=in_ap: h.dma_start(out=o, in_=i),
                       waits=waits, inc=16, kw_sem=sem)

    def flush(self, block):
        self.replay(block)

    def replay(self, block):
        decos = {"T": block.tensor, "V": block.vector, "S": block.scalar,
                 "G": block.gpsimd, "Y": block.sync}
        for e in self.ENGS:
            ops = self.ops[e]
            if not ops:
                continue
            def section(h, ops=ops, e=e):
                for waits, fn, inc, sem in ops:
                    for en, val in waits:
                        h.wait_ge(self.sems[en], val)
                    ins = fn(h)
                    if inc and ins is not None:
                        ins.then_inc(self.sems[sem], inc)
            decos[e](section)
        self.ops = {e: [] for e in self.ENGS}


def build_core(stages=4, dbg=()):
    nc = bass.Bass("TRN2", target_bir_lowering=False)

    xin = nc.dram_tensor("xin", [T, D], F32, kind="ExternalInput")
    wgT = nc.dram_tensor("wgT", [D, D], F32, kind="ExternalInput")
    wvT = nc.dram_tensor("wvT", [D, D], F32, kind="ExternalInput")
    woT = nc.dram_tensor("woT", [D, D], F32, kind="ExternalInput")
    wfcT = nc.dram_tensor("wfcT", [D, FF], F32, kind="ExternalInput")
    wprojT = nc.dram_tensor("wprojT", [FF, D], F32, kind="ExternalInput")
    bgp = nc.dram_tensor("bgp", [128, NC], F32, kind="ExternalInput")
    bvp = nc.dram_tensor("bvp", [128, NC], F32, kind="ExternalInput")
    bgo = nc.dram_tensor("bgo", [2, D], F32, kind="ExternalInput")
    ones16 = nc.dram_tensor("ones16", [1, 128], BF16, kind="ExternalInput")
    onesf = nc.dram_tensor("onesf", [128, 1], F32, kind="ExternalInput")
    onesp0 = nc.dram_tensor("onesp0", [1, 128], F32, kind="ExternalInput")
    identb = nc.dram_tensor("identb", [128, 128], BF16, kind="ExternalInput")
    onesblk = nc.dram_tensor("onesblk", [128, 128], BF16, kind="ExternalInput")
    maskA = nc.dram_tensor("maskA", [128, 256], F32, kind="ExternalInput")
    mask0 = nc.dram_tensor("mask0", [128, 256], F32, kind="ExternalInput")
    nmask = nc.dram_tensor("nmask", [128, 2], F32, kind="ExternalInput")
    constsI = nc.dram_tensor("constsI", [128, 8], F32, kind="ExternalInput")
    out = nc.dram_tensor("out", [T, D], F32, kind="ExternalOutput")
    x2d = nc.dram_tensor("x2d", [T, D], F32)
    wfcC = nc.dram_tensor("wfcC", [128, NF * 1024], BF16)
    wprojC = nc.dram_tensor("wprojC", [128, NF * 1024], BF16)
    dbg_t = {}
    for name, shp in [("dbg_g", [128, NC * GP]), ("dbg_v", [128, (NT + 1) * 1024]),
                       ("dbg_pv", [128, NC * T]),
                      ("dbg_C", [128, 16 * 256]), ("dbg_h", [128, NF * 512])]:
        if name in dbg:
            dbg_t[name] = nc.dram_tensor(name, shp, F32, kind="ExternalOutput")

    es0 = contextlib.ExitStack()
    sb0 = lambda name, shp, dt: es0.enter_context(nc.sbuf_tensor(name, shp, dt))
    con_sb = sb0("con_sb", [128, 8], F32)
    for i, v in enumerate(CONST_VALS):
        nc.const_aps.register(F32, v, con_sb[:, i:i + 1])
    ident_sb = sb0("ident_sb", [128, 128], BF16)
    oblk_sb = sb0("oblk_sb", [128, 128], BF16)
    maskA_sb = sb0("maskA_sb", [128, 256], F32)
    mask0_sb = sb0("mask0_sb", [128, 256], F32)
    bgp_sb = sb0("bgp_sb", [128, NC], F32)
    bvp_sb = sb0("bvp_sb", [128, NC], F32)
    nmask_sb = sb0("nmask_sb", [128, 2], F32)
    bia_b = sb0("bia_b", [1, 2 * D], BF16)
    o16_sb = sb0("o16_sb", [1, 128], BF16)
    onesf_sb = sb0("onesf_sb", [128, 1], F32)
    op0_sb = sb0("op0_sb", [1, 128], F32)

    b = Bld(nc)
    for dst, src in [(con_sb, constsI), (ident_sb, identb), (oblk_sb, onesblk),
                     (maskA_sb, maskA), (mask0_sb, mask0),
                     (nmask_sb, nmask), (bgp_sb, bgp),
                     (bvp_sb, bvp), (o16_sb, ones16),
                     (onesf_sb, onesf), (op0_sb, onesp0)]:
        tok_pre = b.dma("S", "pre", dst[:], src[:])
    tok_pre_g = b.dma("G", "preg", bia_b[:], bgo[:])

    block = es0.enter_context(nc.Block())
    es_b2 = contextlib.ExitStack()
    sb2 = lambda name, shp, dt: es_b2.enter_context(nc.sbuf_tensor(name, shp, dt))
    xa = [sb2(f"xa{i}", [128, D], F32) for i in range(2)]
    xsqd = sb2("xsqd", [128, D], BF16)
    x1n = [sb2(f"x1n{i}", [128, D], BF16) for i in range(2)]
    x1T = sb2("x1T", [128, NC * T], BF16)
    woT_sb = sb2("woT_sb", [128, NC * D], BF16)
    ln_st = [{k: sb2(k + str(j), [128, 1], F32) for k in
              ("meanb", "ssqn", "m2b", "varb", "lnvb", "rstdb", "nbb")}
             for j in range(2)]
    ssum = sb2("ssum", [128, NT], F32)
    ssq = sb2("ssq", [128, NT], F32)
    meanv = sb2("meanv", [128, NT], F32)
    rstdv = sb2("rstdv", [128, NT], F32)
    nbv = sb2("nbv", [128, NT], F32)
    varv = sb2("varv", [128, NT], F32)
    es_bb = contextlib.ExitStack()
    sbb = lambda name, shp, dt: es_bb.enter_context(nc.sbuf_tensor(name, shp, dt))
    g_sb = sbb("g_sb", [128, NC * GP], BF16)
    v_sb = sbb("v_sb", [128, (NT + 1) * 1024], BF16)
    rqv_sb = sbb("rqv_sb", [128, 2 * NT], F32)
    es_a2 = contextlib.ExitStack()
    sba2 = lambda name, shp, dt: es_a2.enter_context(nc.sbuf_tensor(name, shp, dt))
    xpair = [sba2(f"xpair{i}", [128, 2 * D], F32) for i in range(2)]
    wstg = sba2("wstg", [128, 2048], BF16)
    wgT_sb = sba2("wgT_sb", [128, NC * D], BF16)
    wvT_sb = sba2("wvT_sb", [128, NC * D], BF16)
    gtk = [sba2(f"gtk{i}", [128, D], BF16) for i in range(2)]
    gsq = [sba2(f"gsq{i}", [128, D], BF16) for i in range(2)]
    rqx = [sba2(f"rqx{i}", [128, D], BF16) for i in range(2)]
    ghat = [sba2(f"ghat{i}", [128, D], BF16) for i in range(2)]
    sba = sba2

    es_psA = contextlib.ExitStack()
    ps = lambda name, shp=(128, 512): es_psA.enter_context(
        nc.psum_tensor(name, list(shp), F32))
    psT = [ps("psT0"), ps("psT1")]
    psG = [ps("psG0"), ps("psG1")]
    psV = [ps("psV0"), ps("psV1")]

    def wload(dst, src, nchunk, sem):
        src3 = src[:].rearrange("(c p) d -> p c d", p=128)
        dst3 = dst[:].rearrange("p (c d) -> p c d", c=nchunk)
        return b.dma("G", sem, dst3, src3)
    tok_wg = wload(wgT_sb, wgT, NC, "wg")
    tok_wv = wload(wvT_sb, wvT, NC, "wv")
    tok_wo = wload(woT_sb, woT, NC, "wo")

    stg_free = None
    for j in range(16):
        lds = []
        for s in range(2):
            f = 2 * j + s
            lds.append(b.dma("G", f"wstg{s}",
                             wstg[:, 1024 * s:1024 * s + 1024].rearrange(
                                 "p (k f) -> p k f", k=8),
                             wfcT[:, 128 * f:128 * f + 128].rearrange(
                                 "(k p) f -> p k f", p=128),
                             waits=[stg_free]))
        wr = b.dma("G", "wcw", wfcC[:, 2048 * j:2048 * (j + 1)], wstg[:],
                   waits=lds)
        stg_free = wr
    for q in range(16):
        ld = b.dma("G", "wstg", wstg[:].rearrange("p (j d) -> p j d", j=2),
                   wprojT[256 * q:256 * (q + 1), :].rearrange(
                       "(j p) d -> p j d", p=128),
                   waits=[stg_free])
        wr = b.dma("G", "wcw", wprojC[:, 2048 * q:2048 * (q + 1)], wstg[:],
                   waits=[ld])
        stg_free = wr
    tok_wcache = stg_free

    g3 = g_sb[:].rearrange("p (c t) -> p c t", c=NC)
    tok_gpad = b.op("V", lambda h, g3=g3: h.memset(g3[:, :, 0:128], 0.0))
    tok_vpad = b.op("V", lambda h: h.memset(v_sb[:, 0:1024], 0.0))

    xa_free = [None, None]
    x1n_free = [None, None]
    psT_free = [None, None]
    x1T_toks = []
    va_toks = []
    xp_free = [None] * 2
    for tp_ in range(NT // 2):
        j = tp_ % 2
        qeng = "Y" if j == 0 else "S"
        xsrc = xin[256 * tp_:256 * (tp_ + 1), :].rearrange(
            "(a p) d -> p a d", p=128)
        xdst = xpair[j][:].rearrange("p (a d) -> p a d", a=2)
        tok_x = b.dma(qeng, f"xp{j}", xdst, xsrc, waits=[xp_free[j]])
        for half in range(2):
            t = 2 * tp_ + half
            tok_v1 = b.op("V", lambda h, j=j, half=half, t=t: h.scalar_tensor_tensor(
                xsqd[:], xpair[j][:, 1024 * half:1024 * half + 1024], 1.0,
                xpair[j][:, 1024 * half:1024 * half + 1024], ALU.mult, ALU.mult,
                accum_out=ssq[:, t:t + 1]),
                waits=[tok_x, va_toks[-1][0] if va_toks else None])
            tok_s1 = b.op("S", lambda h, j=j, half=half, t=t: h.activation(
                x1n[half][:], xpair[j][:, 1024 * half:1024 * half + 1024],
                AF.Identity, accum_out=ssum[:, t:t + 1]),
                waits=[tok_x, tok_pre, x1n_free[half],
                       va_toks[-1][1] if va_toks else None])
            x1n_free[half] = tok_s1
            va_toks.append((tok_v1, tok_s1))
        xp_free[j] = va_toks[-1]
        xp_free[j] = (va_toks[-2][0], va_toks[-2][1], va_toks[-1][0],
                      va_toks[-1][1])
    tk = b.op("V", lambda h: h.tensor_scalar_mul(meanv[:], ssum[:], 1.0 / D),
              waits=[va_toks[-1]])
    tk2 = b.op("V", lambda h: h.tensor_scalar_mul(varv[:], ssq[:], 1.0 / D),
               waits=[va_toks[-1], tk])
    tk3 = b.op("V", lambda h: h.tensor_mul(nbv[:], meanv[:], meanv[:]),
               waits=[tk, tk2])
    tk4 = b.op("V", lambda h: h.tensor_sub(varv[:], varv[:], nbv[:]),
               waits=[tk3])
    tk5 = b.op("S", lambda h: h.activation(rstdv[:], varv[:], AF.Ln, bias=1e-5),
               waits=[tk4])
    tk6 = b.op("S", lambda h: h.activation(rstdv[:], rstdv[:], AF.Exp,
                                           scale=-0.5), waits=[tk5])
    tok_stats = b.op("V", lambda h: h.scalar_tensor_tensor(
        nbv[:], meanv[:], -1.0, rstdv[:], ALU.mult, ALU.mult),
        waits=[tk6, tk4])
    for t in range(NT):
        i = t % 2
        tok_x = b.dma("Y", f"xb{i}", xa[i][:], xin[128 * t:128 * (t + 1), :],
                      waits=[xa_free[i]])
        tok_x1n = b.op("S", lambda h, i=i, t=t: h.activation(
            x1n[i][:], xa[i][:], AF.Identity, bias=nbv[:, t:t + 1],
            scale=rstdv[:, t:t + 1]), waits=[tok_x, tok_stats, x1n_free[i]])
        xa_free[i] = tok_x1n
        evs = []
        for j in range(2):
            mmt = None
            for c in range(4 * j, 4 * j + 4):
                mmt = b.op("T", lambda h, i=i, j=j, c=c: h.matmul(
                    psT[j][:, 128 * (c % 4):128 * (c % 4) + 128],
                    x1n[i][:, 128 * c:128 * c + 128], ident_sb[:],
                    start=True, stop=True),
                    waits=[tok_x1n, psT_free[j], tok_pre])
            pso3 = psT[j][:].rearrange("p (c q) -> p c q", c=4)
            x1T3 = x1T[:].rearrange("p (c t) -> p c t", c=NC)
            ev = b.op("S", lambda h, j=j, t=t, pso3=pso3, x1T3=x1T3: h.activation(
                x1T3[:, 4 * j:4 * j + 4, 128 * t:128 * t + 128], pso3[:],
                AF.Identity), waits=[mmt])
            psT_free[j] = ev
            evs.append(ev)
        x1n_free[i] = evs[-1]
        x1T_toks.append(evs[-1])
        tok_x1T = evs[-1]

    psG_free = [None, None]
    gtk_free = [None, None]
    gsq_free = [None, None]
    rqx_free = [None, None]
    ghat_free = [None, None]
    g_toks = []
    tok_g = None
    for tt in range(NT):
        ev2 = []
        for nh in range(2):
            gi = (tt * 2 + nh) % 2
            mmt = None
            for k in range(NC):
                mmt = b.op("T", lambda h, gi=gi, tt=tt, nh=nh, k=k: h.matmul(
                    psG[gi][:], x1T[:, 2048 * k + 128 * tt:2048 * k + 128 * tt + 128],
                    wgT_sb[:, 1024 * k + 512 * nh:1024 * k + 512 * nh + 512],
                    start=(k == 0), stop=False),
                    waits=[tok_wg, x1T_toks[tt], psG_free[gi]])
            mmt = b.op("T", lambda h, gi=gi, nh=nh: h.matmul(
                psG[gi][:], o16_sb[:], bia_b[0:1, 512 * nh:512 * nh + 512],
                start=False, stop=True), waits=[mmt, tok_pre, tok_pre_g])
            i2 = tt % 2
            e1 = b.op("S", lambda h, gi=gi, i2=i2, nh=nh: h.activation(
                gtk[i2][:, 512 * nh:512 * nh + 512], psG[gi][:], AF.Identity),
                waits=[mmt, gtk_free[i2] if nh == 0 else None])
            e2 = b.op("S", lambda h, gi=gi, i2=i2, nh=nh: h.activation(
                gsq[i2][:, 512 * nh:512 * nh + 512], psG[gi][:], AF.Square),
                waits=[mmt, gsq_free[i2] if nh == 0 else None])
            psG_free[gi] = e2
            ev2.append((e1, e2))
        i2 = tt % 2
        gq3 = gsq[i2][:].rearrange("p (hh d) -> p hh d", hh=16)
        tn = b.op("V", lambda h, tt=tt, gq3=gq3: h.tensor_reduce(
            rqv_sb[:, 16 * (tt % 2):16 * (tt % 2) + 16],
            gq3, AX, ALU.add), waits=[ev2[0][1], ev2[1][1], rqx_free[tt % 2]])
        tn = b.op("S", lambda h, tt=tt: h.activation(
            rqv_sb[:, 16 * (tt % 2):16 * (tt % 2) + 16],
            rqv_sb[:, 16 * (tt % 2):16 * (tt % 2) + 16], AF.Ln), waits=[tn])
        tn = b.op("S", lambda h, tt=tt: h.activation(
            rqv_sb[:, 16 * (tt % 2):16 * (tt % 2) + 16],
            rqv_sb[:, 16 * (tt % 2):16 * (tt % 2) + 16], AF.Exp, scale=-0.5),
            waits=[tn])
        r3o = rqx[i2][:].rearrange("p (hh d) -> p hh d", hh=16)
        r3i = bass.AP(rqv_sb, 16 * (tt % 2), [[2 * NT, 128], [1, 16], [0, 64]])
        tn = b.op("V", lambda h, r3o=r3o, r3i=r3i: h.tensor_copy(r3o, r3i),
                  waits=[tn])
        tg = b.op("V", lambda h, i2=i2: h.tensor_mul(
            ghat[i2][:], gtk[i2][:], rqx[i2][:]),
            waits=[tn, ev2[0][0], ev2[1][0], ghat_free[i2]])
        gtk_free[i2] = tg
        rqx_free[i2] = tg
        gsq_free[i2] = tn
        evs = []
        for j in range(2):
            mmt = None
            for c in range(4 * j, 4 * j + 4):
                mmt = b.op("T", lambda h, i2=i2, j=j, c=c: h.matmul(
                    psT[j][:, 128 * (c % 4):128 * (c % 4) + 128],
                    ghat[i2][:, 128 * c:128 * c + 128], ident_sb[:],
                    start=True, stop=True),
                    waits=[tg, psT_free[j]])
            pso3 = psT[j][:].rearrange("p (c q) -> p c q", c=4)
            gT3 = g_sb[:].rearrange("p (c t) -> p c t", c=NC)
            ev = b.op("S", lambda h, j=j, tt=tt, pso3=pso3, gT3=gT3: h.activation(
                gT3[:, 4 * j:4 * j + 4, 128 + 128 * tt:128 + 128 * tt + 128],
                pso3[:], AF.Identity), waits=[mmt, tok_gpad if tt == 0 else None])
            psT_free[j] = ev
            evs.append(ev)
        ghat_free[i2] = evs[-1]
        g_toks.append(evs[-1])
        tok_g = evs[-1]

    psV_free = [None, None]
    tok_v = None
    for tt in range(NT):
        for nh in range(2):
            vi = (tt * 2 + nh) % 2
            mmt = None
            for k in range(NC):
                mmt = b.op("T", lambda h, vi=vi, tt=tt, nh=nh, k=k: h.matmul(
                    psV[vi][:], x1T[:, 2048 * k + 128 * tt:2048 * k + 128 * tt + 128],
                    wvT_sb[:, 1024 * k + 512 * nh:1024 * k + 512 * nh + 512],
                    start=(k == 0), stop=(k == NC - 1)),
                    waits=[tok_wv, x1T_toks[tt], psV_free[vi]])
            ev = b.op("S", lambda h, vi=vi, tt=tt, nh=nh: h.activation(
                v_sb[:, 1024 * (tt + 1) + 512 * nh:1024 * (tt + 1) + 512 * nh + 512],
                psV[vi][:], AF.Identity), waits=[mmt, tok_vpad])
            psV_free[vi] = ev
            tok_v = ev

    def dump(name, src_sb, tok, width):
        if name not in dbg_t:
            return
        buf = sba(name + "_b", [128, 512], F32)
        td = None
        for c in range(width // 512):
            tk = b.op("V", lambda h, c=c: h.tensor_copy(
                buf[:], src_sb[:, 512 * c:512 * c + 512]), waits=[tok, td])
            td = b.dma("Y", "dbg", dbg_t[name][:, 512 * c:512 * c + 512],
                       buf[:], waits=[tk])
    dump("dbg_g", g_sb, tok_g, NC * GP // 512 * 512)
    dump("dbg_v", v_sb, tok_v, (NT + 1) * 1024)


    if stages < 2:
        b.op("Y", lambda h: None,
             waits=[(s, b.cnt[s]) for s in b.cnt if s not in "TVS"], inc=0)
        b.flush(block)
        return nc

    b.flush(block)
    es_a2.close()
    es_attR = contextlib.ExitStack()
    sba = lambda name, shp, dt: es_attR.enter_context(
        nc.sbuf_tensor(name, shp, dt, side="right"))
    pv_sb = x1T
    C_sb = [sba(f"C{i}", [128, 16 * 256], F32) for i in range(2)]
    geneb = [sba(f"gene{i}", [128, NT], F32) for i in range(2)]
    etotb = [sba(f"etot{i}", [128, NT], F32) for i in range(2)]
    efb = [sba(f"ef{i}", [128, NT], F32) for i in range(2)]
    denomb = sba("denomb", [128, NT], F32)
    invb = sba("invb", [128, NT], F32)
    invsb = sba("invsb", [128, 1], F32)
    s1b = sba("s1b", [1, 1], F32)
    sib = sba("sib", [128, 1], F32)
    rsib = sba("rsib", [128, 1], F32)
    fitb = [sba(f"fit{i}", [128, NT], F32) for i in range(2)]
    rqb = [sba(f"rqb{i}", [128, 2], F32) for i in range(2)]
    s_sb = [sba(f"s_sb{i}", [128, 2], F32) for i in range(2)]
    ebf = [sba(f"ebf{i}", [128, 256], F32) for i in range(4)]
    em_sb = [sba(f"em{i}", [128, 256], BF16) for i in range(4)]
    PT_sb = [sba(f"PT{i}", [128, 512], BF16) for i in range(2)]

    es_psA.close()
    es_psB = contextlib.ExitStack()
    psb_ = lambda name, shp=(128, 512): es_psB.enter_context(
        nc.psum_tensor(name, list(shp), F32))
    psD = [psb_("psD0", (128, 256)), psb_("psD1", (128, 256))]
    psPT = [psb_("psPT0", (128, 512)), psb_("psPT1", (128, 512))]
    psPV = [psb_("psPV0", (128, 256)), psb_("psPV1", (128, 256))]
    psS = psb_("psS", (128, 512))

    entryB = tuple((s, b.cnt[s]) for s in b.cnt)

    psD_free = [entryB, entryB]
    psPT_free = [entryB] * 4
    psPV_free = [entryB] * 4
    C_free = [None, None]
    gene_free = [None, None]
    etot_free = [None, None]
    fit_free = [None, None]
    ef_free = [None, None]
    ebf_free = [None] * 4
    em_free = [None] * 4
    rq_tok = [None, None]
    PTs_free = [None] * 4
    sq_free = [None, None]
    stats_ps_free = entryB
    pv_toks = []
    tok_nr = [None, None]
    tok_C_h = [None, None]
    tok_fit_h = [None, None]
    c_last = [None, None]
    stat_sm_free = [None]
    pt_mm = [None] * 4
    pv_mm = [None] * 4
    pt_ev = [None, None]
    e_tok = [None] * 4
    em_tok = [None] * 4

    def emit_pass1(h_i, qt):
        hb = h_i % 2
        hc, hp = h_i // 2, 64 * (h_i % 2)
        qb = qt % 2
        md = b.op("T", lambda h, hc=hc, hp=hp, qt=qt, qb=qb: h.matmul(
            psD[qb][:], g_sb[hp:hp + 64, GP * hc + 128 + 128 * qt:GP * hc + 256 + 128 * qt],
            g_sb[hp:hp + 64, GP * hc + 128 * qt:GP * hc + 128 * qt + 256],
            start=True, stop=True),
            waits=[g_toks[qt], tok_gpad, psD_free[qb]])
        t4 = b.op("V", lambda h, hb=hb, qt=qt, qb=qb,
                  msk=(mask0_sb if qt == 0 else maskA_sb): h.scalar_tensor_tensor(
            C_sb[hb][:, 256 * qt:256 * qt + 256], psD[qb][:], 1.0,
            msk[:], ALU.mult, ALU.mult,
            accum_out=geneb[hb][:, qt:qt + 1]),
            waits=[md, C_free[hb] if qt == 0 else None,
                   gene_free[hb] if qt == 0 else None])
        psD_free[qb] = t4
        c_last[hb] = t4

    def emit_stats(h_i):
        nonlocal stats_ps_free
        hb = h_i % 2
        tk = b.op("V", lambda h, hb=hb: h.tensor_scalar(
            denomb[:], geneb[hb][:], 1.0 / 4096.0, 1.0, ALU.mult, ALU.add),
            waits=[c_last[hb], stat_sm_free[0]])
        gene_free[hb] = tk
        tk = b.op("V", lambda h: h.reciprocal(invb[:], denomb[:]), waits=[tk])
        tok_inv = tk
        tk = b.op("V", lambda h: h.reduce_sum(invsb[:], invb[:], AX), waits=[tk])
        m1 = b.op("T", lambda h: h.matmul(psS[0:1, 0:1], invsb[:],
                                          onesf_sb[:], start=True, stop=True),
                  waits=[tk, stats_ps_free, tok_pre])
        e1 = b.op("S", lambda h: h.activation(s1b[:], psS[0:1, 0:1],
                                              AF.Identity), waits=[m1])
        m2 = b.op("T", lambda h: h.matmul(psS[:, 1:2], op0_sb[:], s1b[:],
                                          start=True, stop=True), waits=[e1])
        e2 = b.op("S", lambda h: h.activation(sib[:], psS[:, 1:2],
                                              AF.Identity), waits=[m2])
        stats_ps_free = e2
        tk = b.op("V", lambda h: h.reciprocal(rsib[:], sib[:]), waits=[e2])
        tok_fit_h[hb] = b.op("V", lambda h, hb=hb: h.tensor_scalar(
            fitb[hb][:], invb[:], rsib[:, 0:1], 0.5, ALU.mult, ALU.mult),
            waits=[tk, tok_inv, fit_free[hb]])
        ef_t = b.op("S", lambda h, hb=hb: h.activation(
            efb[hb][:], fitb[hb][:], AF.Exp), waits=[tok_fit_h[hb], ef_free[hb]])
        tok_fit_h[hb] = (tok_fit_h[hb], ef_t)
        stat_sm_free[0] = tok_fit_h[hb]
        tok_C_h[hb] = c_last[hb]

    def emit_e(h_i, qt):
        hb = h_i % 2
        q4 = qt % 4
        e = b.op("S", lambda h, hb=hb, qt=qt, q4=q4: h.activation(
            ebf[q4][:], C_sb[hb][:, 256 * qt:256 * qt + 256], AF.Exp,
            scale=fitb[hb][:, qt:qt + 1], bias=fitb[hb][:, qt:qt + 1],
            accum_out=etotb[hb][:, qt:qt + 1]),
            waits=[tok_fit_h[hb], tok_C_h[hb], ebf_free[q4],
                   etot_free[hb] if qt == 0 else None])
        e_tok[q4] = e
        if qt == NT - 1:
            C_free[hb] = e
            fit_free[hb] = e

    def emit_srq(h_i, qt):
        hb = h_i % 2
        p2 = (qt // 2) % 2
        if qt == 0:
            t1 = b.op("V", lambda h, hb=hb, p2=p2: h.scalar_tensor_tensor(
                s_sb[p2][:, 0:1], efb[hb][:, 0:1], nmask_sb[:, 1:2],
                etotb[hb][:, 0:1], ALU.mult, ALU.add),
                waits=[e_tok[1], sq_free[p2]])
            ts = b.op("V", lambda h, hb=hb, p2=p2: h.scalar_tensor_tensor(
                s_sb[p2][:, 1:2], efb[hb][:, 1:2], nmask_sb[:, 0:1],
                etotb[hb][:, 1:2], ALU.mult, ALU.add),
                waits=[e_tok[1], t1])
        else:
            ts = b.op("V", lambda h, hb=hb, qt=qt, p2=p2: h.scalar_tensor_tensor(
                s_sb[p2][:], efb[hb][:, qt:qt + 2], nmask_sb[:, 0:1],
                etotb[hb][:, qt:qt + 2], ALU.mult, ALU.add),
                waits=[e_tok[(qt + 1) % 4], sq_free[p2]])
        trq = b.op("V", lambda h, p2=p2: h.reciprocal(
            rqb[p2][:], s_sb[p2][:]), waits=[ts])
        rq_tok[p2] = trq
        if qt == NT - 2:
            ef_free[hb] = ts
            etot_free[hb] = ts

    def emit_em(h_i, qt):
        q4 = qt % 4
        p2 = (qt // 2) % 2
        msk = mask0_sb if qt == 0 else maskA_sb
        tem = b.op("V", lambda h, q4=q4, p2=p2, qt=qt, msk=msk: h.scalar_tensor_tensor(
            em_sb[q4][:], ebf[q4][:], rqb[p2][:, qt % 2:qt % 2 + 1], msk[:],
            ALU.mult, ALU.mult), waits=[rq_tok[p2], e_tok[q4], em_free[q4]])
        ebf_free[q4] = tem
        sq_free[p2] = tem
        em_tok[q4] = tem

    def emit_pt(h_i, qt):
        q4 = qt % 4
        pb = q4 // 2
        mpt = None
        for half in range(2):
            mpt = b.op("T", lambda h, q4=q4, pb=pb, half=half: h.matmul(
                psPT[pb][:, 256 * (q4 % 2) + 128 * half:256 * (q4 % 2) + 128 * half + 128],
                em_sb[q4][:, 128 * half:128 * half + 128], ident_sb[:],
                start=True, stop=True),
                waits=[em_tok[q4], psPT_free[q4]])
        em_free[q4] = mpt
        pt_mm[q4] = mpt
        if q4 % 2 == 1:
            ept = b.op("S", lambda h, pb=pb: h.activation(
                PT_sb[pb][:], psPT[pb][:], AF.Identity),
                waits=[mpt, pt_mm[q4 - 1], PTs_free[2 * pb], PTs_free[2 * pb + 1]])
            psPT_free[q4] = ept
            psPT_free[q4 - 1] = ept
            pt_ev[pb] = ept

    def emit_pv(h_i, qt):
        hb = h_i % 2
        hc, hp = h_i // 2, 64 * (h_i % 2)
        q4 = qt % 4
        pb = q4 // 2
        mpv = None
        for half in range(2):
            mpv = b.op("T", lambda h, h_i=h_i, hp=hp, qt=qt, q4=q4, pb=pb, half=half: h.matmul(
                psPV[pb][hp:hp + 64, 128 * (q4 % 2):128 * (q4 % 2) + 128],
                v_sb[:, 1024 * (qt + half) + 64 * h_i:1024 * (qt + half) + 64 * h_i + 64],
                PT_sb[pb][:, 256 * (q4 % 2) + 128 * half:256 * (q4 % 2) + 128 * half + 128],
                start=(half == 0), stop=(half == 1),
                tile_position=(0, hp)),
                waits=[pt_ev[pb], tok_v, psPV_free[q4]])
        PTs_free[q4] = mpv
        if q4 % 2 == 1:
            epv = b.op("S", lambda h, hb=hb, hc=hc, hp=hp, qt=qt, pb=pb: h.activation(
                pv_sb[hp:hp + 64, 2048 * hc + 128 * (qt - 1):2048 * hc + 128 * (qt + 1)],
                psPV[pb][hp:hp + 64, :], AF.Identity,
                bias=bvp_sb[hp:hp + 64, hc:hc + 1]),
                waits=[mpv, pv_mm[q4 - 1]])
            psPV_free[q4 - 1] = epv
            psPV_free[q4] = epv
            pv_toks.append(epv)
        pv_mm[q4] = mpv

    NSTEP = H * NT
    def hq(j):
        return j // NT, j % NT
    for i in range(NSTEP + 23):
        if i < NSTEP:
            emit_pass1(*hq(i))
        if i >= 22 and i - 22 < NSTEP:
            emit_pv(*hq(i - 22))
        if i >= 17 and i - 17 < NSTEP:
            emit_e(*hq(i - 17))
            if (i - 17) % NT % 2 == 1:
                emit_srq(hq(i - 17)[0], hq(i - 17)[1] - 1)
        if i >= 19 and i - 19 < NSTEP:
            emit_em(*hq(i - 19))
        if i >= 20 and i - 20 < NSTEP:
            emit_pt(*hq(i - 20))
        if i < NSTEP and hq(i)[1] == NT - 1:
            emit_stats(hq(i)[0])
    tok_pv = pv_toks[-1]


    if "dbg_pv" in dbg_t:
        buf = sba("dbg_pv_b", [128, 512], F32)
        td = None
        for c in range(NC * T // 512):
            tk = b.op("V", lambda h, c=c: h.tensor_copy(
                buf[:], pv_sb[:, 512 * c:512 * c + 512]), waits=[tok_pv, td])
            td = b.dma("Y", "dbg", dbg_t["dbg_pv"][:, 512 * c:512 * c + 512],
                       buf[:], waits=[tk])
    if "dbg_C" in dbg_t:
        b.dma("Y", "dbg", dbg_t["dbg_C"][:], C_sb[1][:], waits=[tok_pv])

    if stages < 3:
        b.op("Y", lambda h: None, waits=[(s, b.cnt[s]) for s in b.cnt
                                          if s not in "TVS"] + [tok_pv], inc=0)
        b.flush(block)
        return nc

    b.flush(block)
    es_attR.close()
    es_bb.close()
    es_mR = contextlib.ExitStack()
    sba = lambda name, shp, dt: es_mR.enter_context(
        nc.sbuf_tensor(name, shp, dt, side="right"))
    x2nT = sba("x2nT", [128, NC * T], BF16)
    x2t = [sba(f"x2t{i}", [128, D], F32) for i in range(2)]
    wfcb = [sba(f"wfcb{i}", [128, 2048], BF16) for i in range(2)]
    wpb = [sba(f"wpb{i}", [128, 2048], BF16) for i in range(2)]
    h_sb = sba("h_sb", [128, NF * 512], BF16)
    frb = [sba(f"frb{i}", [128, 512], F32) for i in range(2)]
    y_b = [sba(f"y{i}", [128, D], F32) for i in range(4)]
    t1b = sba("t1b", [128, D], F32)
    memb = sba("memb", [128, D], F32)
    spkA = sba("spkA", [128, D], F32)
    spkB = sba("spkB", [128, D], F32)
    accb = sba("accb", [128, D], F32)
    x2r = [sba(f"x2r{i}", [128, D], F32) for i in range(2)]

    xr = xa

    es_psB.close()
    es_psM = contextlib.ExitStack()
    psY = [es_psM.enter_context(nc.psum_tensor(f"psY{i}", [128, 512], F32))
           for i in range(8)]
    entryM = tuple((s, b.cnt[s]) for s in b.cnt)
    x2n = x1n

    xr_free = [entryM, entryM]
    x2t_free = [entryM, entryM]
    x2n_free = [entryM, entryM]
    psY_free = [entryM] * 8
    st2_free = [entryM, entryM]
    x2n_tok = [None, None]
    wfcb_free = [entryM, entryM]
    wpb_free = [entryM, entryM]
    frb_free = [entryM, entryM]
    h_free = entryM
    y_free = [entryM] * 4
    x2r_free = [entryM, entryM]
    scan_free = entryM
    x2nT_toks = []
    x2d_toks = []
    h_toks = [None] * NF
    pend_outs = []

    def emit_b2_tile(tt):
        i = tt % 2
        M = ln_st[i]
        tok_xr = b.dma("Y", f"xr{i}", xr[i][:], xin[128 * tt:128 * (tt + 1), :],
                       waits=[xr_free[i]])
        tk_nh = []
        for nh in range(2):
            pw = 2 + 2 * i + nh
            mmt = None
            for c in range(NC):
                mmt = b.op("T", lambda h, pw=pw, c=c, tt=tt, nh=nh: h.matmul(
                    psY[pw][:], pv_sb[:, 2048 * c + 128 * tt:2048 * c + 128 * tt + 128],
                    woT_sb[:, 1024 * c + 512 * nh:1024 * c + 512 * nh + 512],
                    start=(c == 0), stop=False),
                    waits=[tok_pv, tok_wo, psY_free[pw]])
            mmt = b.op("T", lambda h, pw=pw, nh=nh: h.matmul(
                psY[pw][:], o16_sb[:], bia_b[0:1, D + 512 * nh:D + 512 * nh + 512],
                start=False, stop=True), waits=[mmt, tok_pre_g])
            tv = b.op("V", lambda h, pw=pw, i=i, nh=nh: h.tensor_add(
                x2t[i][:, 512 * nh:512 * nh + 512], psY[pw][:],
                xr[i][:, 512 * nh:512 * nh + 512]),
                waits=[mmt, tok_xr, x2t_free[i] if nh == 0 else None])
            psY_free[pw] = tv
            tk_nh.append(tv)
        xr_free[i] = tk_nh[-1]
        tok_x2 = tk_nh[-1]
        tok_x2w = b.dma("Y", f"x2w{i}", x2d[128 * tt:128 * (tt + 1), :],
                        x2t[i][:], waits=[tok_x2])
        x2d_toks.append(tok_x2w)
        tok_v1 = b.op("V", lambda h, i=i, tt=tt: h.scalar_tensor_tensor(
            xsqd[:], x2t[i][:], 1.0, x2t[i][:], ALU.mult, ALU.mult,
            accum_out=ssq[:, tt:tt + 1]), waits=[tok_x2, entryM])
        tok_s1 = b.op("S", lambda h, i=i, tt=tt: h.activation(
            x2n[i][:], x2t[i][:], AF.Identity, accum_out=ssum[:, tt:tt + 1]),
            waits=[tok_x2, x2n_free[i]])
        tk = b.op("V", lambda h, tt=tt, M=M: h.tensor_scalar_mul(
            M["meanb"][:], ssum[:, tt:tt + 1], 1.0 / D),
            waits=[tok_s1, st2_free[i]])
        tk = b.op("V", lambda h, tt=tt, M=M: h.tensor_scalar_mul(
            M["ssqn"][:], ssq[:, tt:tt + 1], 1.0 / D), waits=[tok_v1, tk])
        tk = b.op("V", lambda h, M=M: h.tensor_mul(M["m2b"][:], M["meanb"][:],
                                                   M["meanb"][:]), waits=[tk])
        tok_var = b.op("V", lambda h, M=M: h.tensor_sub(M["varb"][:], M["ssqn"][:],
                                                        M["m2b"][:]), waits=[tk])
        tk = b.op("S", lambda h, M=M: h.activation(M["lnvb"][:], M["varb"][:], AF.Ln,
                                                   bias=1e-5), waits=[tok_var])
        tok_rstd = b.op("S", lambda h, M=M: h.activation(M["rstdb"][:], M["lnvb"][:],
                                                         AF.Exp, scale=-0.5), waits=[tk])
        tok_nb = b.op("V", lambda h, M=M: h.scalar_tensor_tensor(
            M["nbb"][:], M["meanb"][:], -1.0, M["rstdb"][:], ALU.mult, ALU.mult),
            waits=[tok_rstd])
        tok_x2n = b.op("S", lambda h, i=i, M=M: h.activation(
            x2n[i][:], x2t[i][:], AF.Identity, bias=M["nbb"][:, 0:1],
            scale=M["rstdb"][:, 0:1]), waits=[tok_nb, x2n_free[i], tok_s1])
        st2_free[i] = tok_x2n
        x2t_free[i] = (tok_x2n, tok_x2w)
        x2n_tok[i] = tok_x2n
        last = [tt - 1] if tt > 0 else []
        if tt % 4 == 3:
            last = last + [tt]
        for lt in last:
            li = lt % 2
            evs = []
            for j in range(2):
                pj = 6 + j
                mmt = None
                for c in range(4 * j, 4 * j + 4):
                    mmt = b.op("T", lambda h, li=li, pj=pj, c=c: h.matmul(
                        psY[pj][:, 128 * (c % 4):128 * (c % 4) + 128],
                        x2n[li][:, 128 * c:128 * c + 128], ident_sb[:],
                        start=True, stop=True),
                        waits=[x2n_tok[li], psY_free[pj]])
                pso3 = psY[pj][:].rearrange("p (c q) -> p c q", c=4)
                x2T3 = x2nT[:].rearrange("p (c t) -> p c t", c=NC)
                ev = b.op("S", lambda h, j=j, lt=lt, pso3=pso3, x2T3=x2T3: h.activation(
                    x2T3[:, 4 * j:4 * j + 4, 128 * lt:128 * lt + 128], pso3[:],
                    AF.Identity), waits=[mmt])
                psY_free[6 + j] = ev
                evs.append(ev)
            x2n_free[li] = evs[-1]
            x2nT_toks.append(evs[-1])

    def emit_fc(gi):
        tok_wf = [None, None]
        for f in range(NF):
            pf = f % 2
            fp = f // 2
            if f % 2 == 0:
                tok_wf[fp % 2] = b.dma("G", f"wfs{fp % 2}", wfcb[fp % 2][:],
                                       wfcC[:, 2048 * fp:2048 * (fp + 1)],
                                       waits=[wfcb_free[fp % 2], tok_wcache])
            mmt = None
            for k in range(NC):
                mmt = b.op("T", lambda h, f=f, fp=fp, k=k, gi=gi: h.matmul(
                    psY[f % 2][:],
                    wfcb[fp % 2][:, 1024 * (f % 2) + 128 * k:1024 * (f % 2) + 128 * k + 128],
                    x2nT[:, 2048 * k + 512 * gi:2048 * k + 512 * gi + 512],
                    start=(k == 0), stop=(k == NC - 1)),
                    waits=[tok_wf[fp % 2], x2nT_toks[-1], psY_free[f % 2]])
            if f % 2 == 1:
                wfcb_free[fp % 2] = mmt
            pf = f % 2
            tr = b.op("S", lambda h, pf=pf: h.activation(
                frb[pf][:], psY[pf][:], AF.Relu), waits=[mmt, frb_free[pf]])
            psY_free[pf] = tr
            th = b.op("S", lambda h, pf=pf, f=f: h.activation(
                h_sb[:, 512 * f:512 * f + 512], frb[pf][:], AF.Square),
                waits=[tr, h_free])
            frb_free[pf] = th
            h_toks[f] = th

    def emit_proj_scan(gi):
        nonlocal scan_free, h_free
        while pend_outs:
            tt0, ob0, tout0 = pend_outs.pop(0)
            tst = b.dma("Y", f"out{tt0}", out[128 * tt0:128 * (tt0 + 1), :],
                        ob0[:], waits=[tout0])
            y_free[tt0 % 4] = tst
        pj_done = []
        tok_wp = [None, None]
        for f in range(NF):
            f2 = f // 2
            if f % 2 == 0:
                tok_wp[f2 % 2] = b.dma(
                    "G", f"wps{f2 % 2}", wpb[f2 % 2][:],
                    wprojC[:, 2048 * f2:2048 * (f2 + 1)],
                    waits=[wpb_free[f2 % 2], tok_wcache])
            last = None
            for tti in range(4):
                for nh in range(2):
                    pj = 2 * tti + nh
                    last = b.op("T", lambda h, f=f, f2=f2, tti=tti, nh=nh, pj=pj: h.matmul(
                        psY[pj][:], h_sb[:, 512 * f + 128 * tti:512 * f + 128 * tti + 128],
                        wpb[f2 % 2][:, 1024 * (f % 2) + 512 * nh:1024 * (f % 2) + 512 * nh + 512],
                        start=(f == 0), stop=(f == NF - 1)),
                        waits=[tok_wp[f2 % 2], h_toks[f],
                               psY_free[pj] if f == 0 else None])
                    if f == NF - 1:
                        pj_done.append(last)
            if f % 2 == 1:
                wpb_free[f2 % 2] = last
        h_free = ("T", b.cnt["T"])
        for tti in range(4):
            tt = 4 * gi + tti
            i = tt % 2
            i4 = tt % 4
            ev = []
            for nh in range(2):
                pj = 2 * tti + nh
                e = b.op("S", lambda h, pj=pj, i4=i4, nh=nh: h.activation(
                    y_b[i4][:, 512 * nh:512 * nh + 512], psY[pj][:],
                    AF.Identity), waits=[pj_done[pj], y_free[i4]])
                psY_free[pj] = e
                ev.append(e)
            tok_y = ev[-1]
            tok_x2r = b.dma("Y", f"x2r{i}", x2r[i][:],
                            x2d[128 * tt:128 * (tt + 1), :],
                            waits=[x2d_toks[tt], x2r_free[i]])
            y = y_b[i4]
            tk = b.op("V", lambda h, y=y: h.tensor_single_scalar(
                spkA[:], y[:], 1.0, ALU.is_gt), waits=[tok_y, scan_free])
            tmem = None
            acc_t = None
            for step in range(2, 6):
                s_cur = spkB if step % 2 == 0 else spkA
                s_prv = spkA if step % 2 == 0 else spkB
                src_ = y if step == 2 else memb
                tk = b.op("V", lambda h, src_=src_, y=y: h.scalar_tensor_tensor(
                    t1b[:], src_[:], 0.9, y[:], ALU.mult, ALU.add),
                    waits=[tk, tmem])
                tmem = b.op("V", lambda h, s_prv=s_prv: h.tensor_sub(
                    memb[:], t1b[:], s_prv[:]), waits=[tk])
                tk = b.op("V", lambda h, s_cur=s_cur: h.tensor_single_scalar(
                    s_cur[:], memb[:], 1.0, ALU.is_gt), waits=[tmem])
                if step == 2:
                    acc_t = b.op("V", lambda h: h.tensor_add(
                        accb[:], spkA[:], spkB[:]), waits=[tk])
                else:
                    acc_t = b.op("V", lambda h, s_cur=s_cur: h.tensor_add(
                        accb[:], accb[:], s_cur[:]), waits=[tk, acc_t])
            ob = y_b[i4]
            tout = b.op("V", lambda h, i=i, ob=ob: h.scalar_tensor_tensor(
                ob[:], accb[:], 0.2, x2r[i][:], ALU.mult, ALU.add),
                waits=[acc_t, tok_x2r])
            scan_free = tout
            x2r_free[i] = tout
            pend_outs.append((tt, ob, tout))

    for tt in range(4):
        emit_b2_tile(tt)
    emit_fc(0)
    for tt in range(4, 8):
        emit_b2_tile(tt)
    emit_proj_scan(0)
    emit_fc(1)
    for tt in range(8, 12):
        emit_b2_tile(tt)
    emit_proj_scan(1)
    emit_fc(2)
    for tt in range(12, 16):
        emit_b2_tile(tt)
    emit_proj_scan(2)
    emit_fc(3)
    emit_proj_scan(3)
    while pend_outs:
        tt0, ob0, tout0 = pend_outs.pop(0)
        tst = b.dma("Y", f"out{tt0}", out[128 * tt0:128 * (tt0 + 1), :],
                    ob0[:], waits=[tout0])
        y_free[tt0 % 4] = tst


    b.op("Y", lambda h: None,
         waits=[(s, b.cnt[s]) for s in b.cnt if s not in "TVS"], inc=0)
    b.flush(block)
    return nc


def prep_inputs(xb, Wg, bg, Wv, bv, Wo, bo, W_fc, W_proj):
    bf = ml_dtypes.bfloat16
    f32 = np.float32
    consts = np.tile(np.array([CONST_VALS + [0.0]], dtype=f32), (128, 1))
    r = np.arange(128)[:, None]
    c = np.arange(256)[None, :]
    mA = ((c >= r) & (c <= r + 128)).astype(f32)
    m0 = mA.copy()
    m0[:, :128] = 0.0
    oblk = np.zeros((128, 128), dtype=np.float32)
    for dc in range(NC):
        for p in range(128):
            oblk[p, 16 * dc + 2 * dc + (1 if p >= 64 else 0)] = 1.0
    return {
        "xin": np.ascontiguousarray(xb, dtype=f32),
        "wgT": np.ascontiguousarray(Wg.T, dtype=f32),
        "wvT": np.ascontiguousarray(Wv.T, dtype=f32),
        "woT": np.ascontiguousarray(Wo.T, dtype=f32),
        "wfcT": np.ascontiguousarray(W_fc.T, dtype=f32),
        "wprojT": np.ascontiguousarray(W_proj.T, dtype=f32),
        "bgp": np.ascontiguousarray(bg.reshape(NC, 128).T, dtype=f32),
        "bvp": np.ascontiguousarray(
            bv.reshape(NC, 2, 64).transpose(1, 2, 0).reshape(128, NC), dtype=f32),
        "bgo": np.ascontiguousarray(np.stack([bg, bo]), dtype=f32),
        "ones16": np.ones((1, 128), dtype=bf),
        "onesf": np.ones((128, 1), dtype=f32),
        "onesp0": np.ones((1, 128), dtype=f32),
        "identb": np.eye(128).astype(bf),
        "onesblk": oblk.astype(bf),
        "maskA": mA, "mask0": m0,
        "nmask": np.stack([np.full(128, -127.0, np.float32),
                           -(255.0 - np.arange(128, dtype=np.float32))],
                          axis=1),
        "constsI": consts,
    }




from concourse.bass_utils import run_bass_kernel_spmd


def kernel(L0_attn_x, L0_attn_Wg, L0_attn_bg, L0_attn_Wv, L0_attn_bv,
           L0_attn_Wo, L0_attn_bo, L0_ffn_W_fc, L0_ffn_W_proj):
    x = np.ascontiguousarray(np.asarray(L0_attn_x), dtype=np.float32)
    base = prep_inputs(x[0], np.asarray(L0_attn_Wg), np.asarray(L0_attn_bg),
                       np.asarray(L0_attn_Wv), np.asarray(L0_attn_bv),
                       np.asarray(L0_attn_Wo), np.asarray(L0_attn_bo),
                       np.asarray(L0_ffn_W_fc), np.asarray(L0_ffn_W_proj))
    in_maps = []
    for c in range(8):
        m = dict(base)
        m["xin"] = x[c]
        in_maps.append(m)
    nc = build_core(stages=4, dbg=())
    res = run_bass_kernel_spmd(nc, in_maps, core_ids=list(range(8)))
    return np.stack([np.asarray(r["out"], dtype=np.float32)
                     for r in res.results], axis=0)
